# version 06
# baseline (speedup 1.0000x reference)
import os
import numpy as np
import ml_dtypes
import concourse.bass as bass
import concourse.mybir as mybir
from concourse.bass_utils import run_bass_kernel_spmd

F32 = mybir.dt.float32
BF16 = mybir.dt.bfloat16
I8 = mybir.dt.int8
AF = mybir.ActivationFunctionType
ALU = mybir.AluOpType
AXX = mybir.AxisListType.X
BF = ml_dtypes.bfloat16

BB, LL, DD, NN, RK, HH = 2, 16384, 2048, 16, 128, 8192
TOK = 4096
QT = 512
NT = TOK // QT
QS = 512
NCH = LL // QS
NJ = 4
HB = 64
KC = 16
DB = 16
GROUPS = [[0, 1, 2, 3], [4, 5, 6, 7]]
CCROWS = 162
EPS = 1e-5
OFFS = 4


class Sem:
    def __init__(self, h):
        self.h = h
        self.n = 0

    def inc(self, by=1):
        self.n += by
        return self.n


class PS:

    def __init__(self, h0, h1):
        self.h = (h0, h1)
        self.n = [0, 0]

    def inc(self, p):
        self.n[p] += 16
        return self.n[p]


def finish(nc, ev):
    with nc.Block() as blk:
        @blk.sync
        def _(sync):
            for fn in ev["sync"]:
                fn(sync)

        @blk.tensor
        def _(tensor):
            for fn in ev["tensor"]:
                fn(tensor)

        @blk.vector
        def _(vector):
            for fn in ev["vector"]:
                fn(vector)

        @blk.scalar
        def _(scalar):
            for fn in ev["scalar"]:
                fn(scalar)

        @blk.gpsimd
        def _(gpsimd):
            for fn in ev["gpsimd"]:
                fn(gpsimd)
    return nc


def build_program():
    nc = bass.Bass("TRN2", target_bir_lowering=False, debug=False,
                   detect_race_conditions=False)

    d_xtok = nc.dram_tensor("x_tok", [TOK, DD], F32, kind="ExternalInput").ap()
    d_xtokT = nc.dram_tensor("x_tokT", [DD, TOK], F32, kind="ExternalInput").ap()
    d_xdslT = nc.dram_tensor("x_dslT", [NJ * 128, LL], F32, kind="ExternalInput").ap()
    d_wgu = nc.dram_tensor("wgu", [HB, 128, 4096], I8, kind="ExternalInput").ap()
    d_wd = nc.dram_tensor("wd", [DB, 128, HH], I8, kind="ExternalInput").ap()
    d_wxdt = nc.dram_tensor("wxdt", [KC, 128, 128], F32, kind="ExternalInput").ap()
    d_wxbm = nc.dram_tensor("wxbm", [KC, 128, 16], F32, kind="ExternalInput").ap()
    d_wxcm = nc.dram_tensor("wxcm", [KC, 128, 16], F32, kind="ExternalInput").ap()
    d_wdtT = nc.dram_tensor("wdtT", [128, NJ * 128], F32, kind="ExternalInput").ap()
    d_A = nc.dram_tensor("A_r", [128, NJ * NN], F32, kind="ExternalInput").ap()
    d_bdt = nc.dram_tensor("bdt_r", [128, NJ], F32, kind="ExternalInput").ap()
    d_D = nc.dram_tensor("D_r", [128, NJ], F32, kind="ExternalInput").ap()
    d_sg = nc.dram_tensor("sg_r", [128, HB], F32, kind="ExternalInput").ap()
    d_su = nc.dram_tensor("su_r", [128, HB], F32, kind="ExternalInput").ap()
    d_sd = nc.dram_tensor("sd_r", [128, DB], F32, kind="ExternalInput").ap()
    d_id = nc.dram_tensor("ident", [128, 128], BF16, kind="ExternalInput").ap()
    d_eps = nc.dram_tensor("eps_c", [128, 1], F32, kind="ExternalInput").ap()
    d_out = nc.dram_tensor("outT", [DD, TOK], F32, kind="ExternalOutput").ap()
    DBGOUT = os.environ.get("K_DBGOUT", "")
    d_dbg = (nc.dram_tensor("dbg", [2048, 4096], BF16, kind="ExternalOutput").ap()
             if DBGOUT else None)

    cc1_in = nc.dram_tensor("cc1_in", [CCROWS, TOK], BF16)
    cc1_out = nc.dram_tensor("cc1_out", [4 * CCROWS, TOK], BF16)
    ccs_in = [nc.dram_tensor(f"ccs_in{q}", [NJ * 128, 4096], BF16) for q in range(4)]
    ccs_out = nc.dram_tensor("ccs_out", [4 * 2048, 4096], BF16)
    d_ghbm = nc.dram_tensor("g_hbm", [NT * DB * 128, QT], BF16)
    bar_in = nc.dram_tensor("bar_in", [128, 1], F32)
    bar_out = nc.dram_tensor("bar_out", [128, 1], F32)
    s_mine = nc.dram_tensor("s_mine", [2048, 4096], BF16)
    d_phbm = nc.dram_tensor("p_hbm", [DB * 128, QT], BF16)

    import contextlib
    es = contextlib.ExitStack()
    sbt = lambda n, s, d: es.enter_context(nc.sbuf_tensor(n, s, d))
    psm = lambda n, s, d: es.enter_context(nc.psum_tensor(n, s, d))

    with es:
        ident = sbt("ident_s", [128, 128], BF16)
        wxdt_s = sbt("wxdt_s", [128, KC * 128], BF16)
        wxbm_s = sbt("wxbm_s", [128, KC * 16], BF16)
        wxcm_s = sbt("wxcm_s", [128, KC * 16], BF16)
        wdtT_s = sbt("wdtT_s", [128, NJ * 128], BF16)
        A_s = sbt("A_s", [128, NJ * NN], F32)
        bdt_s = sbt("bdt_s", [128, NJ], F32)
        D_s = sbt("D_s", [128, NJ], F32)
        sg_s = sbt("sg_s", [128, HB], F32)
        su_s = sbt("su_s", [128, HB], F32)
        sd_s = sbt("sd_s", [128, DB], F32)
        eps_s = sbt("eps_s", [128, 1], F32)
        x_c = sbt("x_c", [128, DD], F32)
        xsq = sbt("xsq", [128, DD], BF16)
        xn_tm = sbt("xn_tm", [128, DD], BF16)
        stat = sbt("stat", [128, 8], F32)
        stats_all = sbt("stats_all", [128, 64], F32)
        stats_bf = sbt("stats_bf", [128, 64], BF16)
        stT_sb = sbt("stT_sb", [64, 128], BF16)
        xnT = [sbt(f"xnT{i}", [128, KC * QT], BF16) for i in range(2)]
        dt_ev = [sbt(f"dt_ev{i}", [128, QT], BF16) for i in range(2)]
        bc_ev = [sbt(f"bc_ev{i}", [48, QT], BF16) for i in range(2)]
        xT_s = [sbt(f"xT{i}", [128, QS], F32) for i in range(2)]
        xtmp = sbt("xtmp", [128, QS], F32)
        mrB = [sbt(f"mrB{i}", [128, QS], BF16) for i in range(2)]
        rsB = [sbt(f"rsB{i}", [128, QS], BF16) for i in range(2)]
        dtv = [sbt(f"dtv{i}", [128, QS], BF16) for i in range(2)]
        esp = sbt("esp", [128, QS], F32)
        delta = sbt("delta", [128, NJ * QS], F32)
        xnsl = sbt("xnsl", [128, NJ * QS], BF16)
        u_s = sbt("u_s", [128, NJ * QS], BF16)
        BmA = sbt("BmA", [128, NN * QS], BF16)
        CmA = sbt("CmA", [128, NN * QS], BF16)
        ad_s = [sbt(f"ad{i}", [128, QS], F32) for i in range(2)]
        bx_s = [sbt(f"bx{i}", [128, QS], BF16) for i in range(2)]
        h_s = [sbt(f"h{i}", [128, QS], BF16) for i in range(3)]
        w_s = [sbt(f"w{i}", [128, QS], BF16) for i in range(2)]
        hcar = sbt("hcar", [128, NJ * NN], F32)
        s_st = [sbt(f"s_st{i}", [128, QS], BF16) for i in range(2)]
        wgu_s = [sbt(f"wgu{i}", [128, 4096], BF16) for i in range(2)]
        wd_s = [sbt(f"wd{i}", [128, 2048], BF16) for i in range(2)]
        gate_s = [sbt(f"gate{i}", [128, QT], BF16) for i in range(2)]
        up_s = [sbt(f"up{i}", [128, QT], BF16) for i in range(2)]
        fuse = [sbt(f"fuse{i}", [128, 16 * QT], BF16) for i in range(2)]
        pstg = [sbt(f"pstg{i}", [128, QT], BF16) for i in range(2)]
        pldg = [sbt(f"pldg{i}", [128, QT], BF16) for i in range(2)]
        gpre = [sbt(f"gpre{i}", [128, QT], F32) for i in range(2)]
        g_st = [sbt(f"g_st{i}", [128, QT], BF16) for i in range(2)]
        s_ld = bx_s
        g_ld = h_s[:2]
        xT_ld = xT_s
        sg4 = w_s
        of_s = ad_s

        pg = psm("pg", [128, QT], F32)
        pu = psm("pu", [128, QT], F32)
        pdn = [psm(f"pdn{i}", [128, QT], F32) for i in range(2)]
        pxy = [psm(f"pxy{i}", [128, QS], F32) for i in range(2)]
        ptr = [psm(f"ptr{i}", [128, 1024], BF16) for i in range(2)]

        semnames = [
            "ldc", "xld", "sq", "stat_", "rstd", "xn", "tr", "trv", "xnt",
            "xp", "xpe", "ag1",
            "dtp", "sp", "xnsl_", "u_",
            "ad", "bx", "scan", "wmul", "yi", "sY", "ags",
            "gug", "guu", "ge", "ue", "fu", "dn", "pe_",
            "gsil",
            "sg_", "ofa", "smv", "ldcg", "vch", "sch", "sgo",
        ]
        S = {nm: Sem(es.enter_context(nc.semaphore(nm))) for nm in semnames}
        psnames = ["ctr", "xt", "dtld", "bcb", "bca", "sdma",
                   "wg", "wdl", "pst", "pld", "gdma",
                   "sld", "gld", "xtld", "out"]
        for nm in psnames:
            S[nm] = PS(es.enter_context(nc.semaphore(nm + "0")),
                       es.enter_context(nc.semaphore(nm + "1")))
        ev = {e: [] for e in ("sync", "tensor", "vector", "scalar", "gpsimd")}

        def add(eng, fn):
            ev[eng].append(fn)

        def W(eng, sem, val):
            if val > 0:
                add(eng, lambda e, s=sem.h, v=val: e.wait_ge(s, v))

        tr_cnt = [0]
        ctr_after_tile = {}
        pst_after = {}
        gdma_after = {}
        out_after = {}
        pld_after = {}
        P4ST = {}

        def emit_consts():
            for dst, src in ((A_s, d_A), (bdt_s, d_bdt), (D_s, d_D),
                             (sg_s, d_sg), (su_s, d_su), (sd_s, d_sd)):
                S["ldc"].inc(16)
                add("sync", lambda e, d=dst, s=src: e.dma_start(out=d[:, :], in_=s).then_inc(S["ldc"].h, 16))
            S["ldc"].inc(16)
            add("sync", lambda e: e.dma_start(out=ident[:, :], in_=d_id).then_inc(S["ldc"].h, 16))
            S["ldc"].inc(16)
            add("sync", lambda e: e.dma_start(out=eps_s[:, :], in_=d_eps).then_inc(S["ldc"].h, 16))
            weps = S["ldc"].n
            S["ldc"].inc(16)
            add("sync", lambda e, w=weps:
                (e.wait_ge(S["ldc"].h, w),
                 e.dma_start(out=bar_in.ap()[:, :], in_=eps_s[:, 0:1]).then_inc(S["ldc"].h, 16)))
            S["ldcg"].inc(16)
            add("gpsimd", lambda e: e.dma_start(out=wdtT_s[:, :], in_=d_wdtT).then_inc(S["ldcg"].h, 16))
            for dst, src, f in ((wxdt_s, d_wxdt, 128), (wxbm_s, d_wxbm, 16), (wxcm_s, d_wxcm, 16)):
                for i in range(KC):
                    S["ldcg"].inc(16)
                    add("gpsimd", lambda e, d=dst, s=src, i=i, f=f:
                        e.dma_start(out=d[:, i * f:(i + 1) * f], in_=s[i]).then_inc(S["ldcg"].h, 16))
            add("vector", lambda e: e.memset(hcar[:, :], 0.0))
            for eng in ("tensor", "vector", "scalar"):
                W(eng, S["ldc"], S["ldc"].n)
                W(eng, S["ldcg"], S["ldcg"].n)

        def emit_ln_chunk(k, ch, xnT_dst, p1a, last_of_tile=False):
            col = k * 4 + ch
            wv_sq, wv_xn = S["sq"].n, S["xn"].n
            v = S["xld"].inc(16)
            add("sync", lambda e, col=col, wsq=wv_sq, wxn=wv_xn:
                (e.wait_ge(S["sq"].h, wsq) if wsq else None,
                 e.wait_ge(S["xn"].h, wxn) if wxn else None,
                 e.dma_start(out=x_c[:, :], in_=d_xtok[col * 128:(col + 1) * 128, :])
                 .then_inc(S["xld"].h, 16)))
            wtr = 16 * tr_cnt[0]
            if p1a:
                W("scalar", S["xld"], v)
                sqv = S["sq"].inc()
                add("scalar", lambda e: e.activation(xsq[:, :], x_c[:, :], AF.Square).then_inc(S["sq"].h, 1))
                W("vector", S["xld"], v)
                W("vector", S["sq"], sqv)

                def vc(fn):
                    w = S["vch"].n
                    S["vch"].inc()
                    add("vector", lambda e, w=w, fn=fn:
                        (e.wait_ge(S["vch"].h, w) if w else None, fn(e).then_inc(S["vch"].h, 1)))

                vc(lambda e: e.reduce_sum(stat[:, 0:1], x_c[:, :], AXX))
                vc(lambda e: e.reduce_sum(stat[:, 1:2], xsq[:, :], AXX))
                vc(lambda e: e.tensor_scalar_mul(stat[:, 2:3], stat[:, 0:1], 1.0 / DD))
                vc(lambda e: e.tensor_scalar_mul(stat[:, 3:4], stat[:, 1:2], 1.0 / DD))
                vc(lambda e: e.tensor_tensor(stat[:, 0:1], stat[:, 2:3], stat[:, 2:3], ALU.mult))
                vc(lambda e: e.tensor_tensor(stat[:, 3:4], stat[:, 3:4], stat[:, 0:1], ALU.subtract))
                W("scalar", S["vch"], S["vch"].n)
                wsc = S["sch"].n
                S["sch"].inc()
                add("scalar", lambda e, w=wsc: (e.wait_ge(S["sch"].h, w) if w else None,
                                                e.activation(stat[:, 4:5], stat[:, 3:4], AF.Ln, bias=eps_s[:, 0:1])
                                                .then_inc(S["sch"].h, 1)))
                wsc = S["sch"].n
                S["sch"].inc()
                add("scalar", lambda e, w=wsc: (e.wait_ge(S["sch"].h, w),
                                                e.activation(stat[:, 4:5], stat[:, 4:5], AF.Exp, scale=-0.5)
                                                .then_inc(S["sch"].h, 1)))
                W("vector", S["sch"], S["sch"].n)
                W("vector", S["tr"], wtr)
                W("vector", S["vch"], S["vch"].n)
                xnv = S["xn"].inc()
                add("vector", lambda e: e.tensor_scalar(xn_tm[:, :], x_c[:, :], stat[:, 2:3], stat[:, 4:5],
                                                        ALU.subtract, ALU.mult).then_inc(S["xn"].h, 1))
                vc(lambda e: e.tensor_tensor(stat[:, 5:6], stat[:, 2:3], stat[:, 4:5], ALU.mult))
                vc(lambda e, col=col: e.tensor_copy(stats_all[:, col:col + 1], stat[:, 5:6]))
                W("vector", S["vch"], S["vch"].n)
                sv2 = S["stat_"].inc()
                add("vector", lambda e, col=col: e.tensor_copy(stats_all[:, 32 + col:33 + col], stat[:, 4:5])
                    .then_inc(S["stat_"].h, 1))
            else:
                W("vector", S["xld"], v)
                W("vector", S["tr"], wtr)
                xnv = S["xn"].inc()
                add("vector", lambda e, col=col:
                    e.tensor_scalar(xn_tm[:, :], x_c[:, :], stats_all[:, 32 + col:33 + col],
                                    stats_all[:, col:col + 1], ALU.mult, ALU.subtract).then_inc(S["xn"].h, 1))
            if p1a:
                wdst = ("xp", S["xp"].n)
            else:
                wdst = ("guu", 64 * (k - 1)) if k >= 2 else ("guu", 0)
            for i in range(KC):
                t = tr_cnt[0]
                tr_cnt[0] += 1
                tv = S["tr"].inc(16)
                add("tensor", lambda e, i=i, t=t, xnv=xnv, wd=wdst:
                    (e.wait_ge(S["xn"].h, xnv),
                     e.wait_ge(S["trv"].h, t - 1) if t >= 1 else None,
                     e.wait_ge(S[wd[0]].h, wd[1]) if wd[1] else None,
                     e.transpose(ptr[t % 2][:, 0:128], xn_tm[:, i * 128:(i + 1) * 128], ident[:, :])
                     .then_inc(S["tr"].h, 16)))
                S["trv"].inc()
                d0 = i * QT + ch * 128
                fin = last_of_tile and i == KC - 1
                add("vector", lambda e, t=t, tv=tv, d0=d0, dst=xnT_dst, wd=wdst, fin=fin:
                    (e.wait_ge(S["tr"].h, tv),
                     e.wait_ge(S[wd[0]].h, wd[1]) if wd[1] else None,
                     (e.tensor_copy(dst[:, d0:d0 + 128], ptr[t % 2][:, 0:128]).then_inc(S["xnt"].h, 1)) if fin
                     else e.tensor_copy(dst[:, d0:d0 + 128], ptr[t % 2][:, 0:128]).then_inc(S["trv"].h, 1)))
                if fin:
                    S["xnt"].inc()
                    add("vector", lambda e: e.engine_nop().then_inc(S["trv"].h, 1))

        def emit_p1a_tile(k):
            xnT0 = fuse[0]
            for ch in range(4):
                emit_ln_chunk(k, ch, xnT0, p1a=True)
            xv = S["trv"].n
            wxpe = S["xpe"].n
            if os.environ.get("K_NOXPROJ") == "1":
                ctr_after_tile[k] = S["ctr"].n[k % 2]
                return
            for i in range(KC):
                last = i == KC - 1
                if last:
                    S["xp"].inc()
                add("tensor", lambda e, i=i, last=last, xv=xv, wxpe=wxpe:
                    (e.wait_ge(S["trv"].h, xv) if i == 0 else None,
                     e.wait_ge(S["xpe"].h, wxpe) if (i == 0 and wxpe) else None,
                     e.matmul(pxy[0][:, :], wxdt_s[:, i * 128:(i + 1) * 128], fuse[0][:, i * QT:(i + 1) * QT],
                              start=(i == 0), stop=last).then_inc(S["xp"].h, 1) if last else
                     e.matmul(pxy[0][:, :], wxdt_s[:, i * 128:(i + 1) * 128], fuse[0][:, i * QT:(i + 1) * QT],
                              start=(i == 0), stop=last)))
            for i in range(KC):
                last = i == KC - 1
                add("tensor", lambda e, i=i, last=last:
                    e.matmul(pxy[1][0:16, :], wxbm_s[:, i * 16:(i + 1) * 16], fuse[0][:, i * QT:(i + 1) * QT],
                             start=(i == 0), stop=last))
            for i in range(KC):
                last = i == KC - 1
                if last:
                    S["xp"].inc()
                add("tensor", lambda e, i=i, last=last:
                    e.matmul(pxy[1][32:48, :], wxcm_s[:, i * 16:(i + 1) * 16], fuse[0][:, i * QT:(i + 1) * QT],
                             start=(i == 0), stop=last).then_inc(S["xp"].h, 1) if last else
                    e.matmul(pxy[1][32:48, :], wxcm_s[:, i * 16:(i + 1) * 16], fuse[0][:, i * QT:(i + 1) * QT],
                             start=(i == 0), stop=last))
            vxp = S["xp"].n
            wctr = ctr_after_tile.get(k - 2, 0)
            e1 = S["xpe"].inc()
            add("vector", lambda e, vxp=vxp, k=k, w=wctr:
                (e.wait_ge(S["xp"].h, vxp - 1),
                 e.wait_ge(S["ctr"].h[k % 2], w) if w else None,
                 e.tensor_copy(dt_ev[k % 2][:, :], pxy[0][:, :]).then_inc(S["xpe"].h, 1)))
            e2 = S["xpe"].inc()
            add("vector", lambda e, vxp=vxp, k=k:
                (e.wait_ge(S["xp"].h, vxp),
                 e.tensor_copy(bc_ev[k % 2][0:16, :], pxy[1][0:16, :]).then_inc(S["xpe"].h, 1)))
            e3 = S["xpe"].inc()
            add("vector", lambda e, k=k:
                e.tensor_copy(bc_ev[k % 2][32:48, :], pxy[1][32:48, :]).then_inc(S["xpe"].h, 1))
            for (we, rows, src) in ((e1, (0, 128), dt_ev[k % 2]), (e2, (128, 144), None), (e3, (144, 160), None)):
                S["ctr"].inc(k % 2)
                if src is not None:
                    add("sync", lambda e, k=k, we=we, rows=rows, src=src:
                        (e.wait_ge(S["xpe"].h, we),
                         e.dma_start(out=cc1_in.ap()[rows[0]:rows[1], k * QT:(k + 1) * QT], in_=src[:, :])
                         .then_inc(S["ctr"].h[k % 2], 16)))
                elif rows[0] == 128:
                    add("sync", lambda e, k=k, we=we:
                        (e.wait_ge(S["xpe"].h, we),
                         e.dma_start(out=cc1_in.ap()[128:144, k * QT:(k + 1) * QT], in_=bc_ev[k % 2][0:16, :])
                         .then_inc(S["ctr"].h[k % 2], 16)))
                else:
                    add("sync", lambda e, k=k, we=we:
                        (e.wait_ge(S["xpe"].h, we),
                         e.dma_start(out=cc1_in.ap()[144:160, k * QT:(k + 1) * QT], in_=bc_ev[k % 2][32:48, :])
                         .then_inc(S["ctr"].h[k % 2], 16)))
            ctr_after_tile[k] = S["ctr"].n[k % 2]

        def emit_p1b_tile(k):
            for ch in range(4):
                emit_ln_chunk(k, ch, xnT[k % 2], p1a=False, last_of_tile=(ch == 3))

        def emit_ag1():
            v0, v1 = S["ctr"].n[0], S["ctr"].n[1]
            S["ag1"].inc()
            add("gpsimd", lambda e, v0=v0, v1=v1:
                (e.wait_ge(S["ctr"].h[0], v0), e.wait_ge(S["ctr"].h[1], v1) if v1 else None,
                 e.collective_compute("AllGather", ALU.bypass, replica_groups=GROUPS,
                                      ins=[cc1_in.ap().opt()], outs=[cc1_out.ap().opt()])
                 .then_inc(S["ag1"].h, 1)))

        def emit_p2_chunk(cc, cb=None):
            t0 = cc * QS
            r = cc // 8
            lc = t0 - r * 4096
            base = r * CCROWS
            if cc == 0:
                W("scalar", S["ag1"], 1)
                W("tensor", S["ag1"], 1)
            v_dt = S["dtld"].inc(cc % 2)
            add("scalar", lambda e, base=base, lc=lc, cc=cc, w=max(0, 4 * (cc - 1)):
                (e.wait_ge(S["dtp"].h, w) if w else None,
                 e.dma_start(out=dtv[cc % 2][:, :], in_=cc1_out.ap()[base:base + 128, lc:lc + QS])
                 .then_inc(S["dtld"].h[cc % 2], 16)))
            wxnsl = max(0, 4 * (cc - 1))
            for row, dst in ((160, mrB), (161, rsB)):
                S["bcb"].inc(cc % 2)
                add("scalar", lambda e, base=base, row=row, lc=lc, dst=dst, cc=cc, w=wxnsl:
                    (e.wait_ge(S["xnsl_"].h, w) if w else None,
                     e.dma_start(out=dst[cc % 2][:, :],
                                 in_=cc1_out.ap()[base + row:base + row + 1, lc:lc + QS].partition_broadcast(128))
                     .then_inc(S["bcb"].h[cc % 2], 16)))
            v_mr = S["bcb"].n[cc % 2]
            for n in range(NN):
                wbx = max(0, 64 * (cc - 1) + 48 + n + 1) if cc > 0 else 0
                S["bca"].inc(cc % 2)
                add("scalar", lambda e, base=base, n=n, lc=lc, cc=cc, w=wbx:
                    (e.wait_ge(S["bx"].h, w) if w else None,
                     e.dma_start(out=BmA[:, n * QS:(n + 1) * QS],
                                 in_=cc1_out.ap()[base + 128 + n:base + 129 + n, lc:lc + QS].partition_broadcast(128))
                     .then_inc(S["bca"].h[cc % 2], 16)))
                S["bca"].inc(cc % 2)
                add("scalar", lambda e, base=base, n=n, lc=lc, cc=cc, w=wbx:
                    (e.wait_ge(S["wmul"].h, w) if w else None,
                     e.dma_start(out=CmA[:, n * QS:(n + 1) * QS],
                                 in_=cc1_out.ap()[base + 144 + n:base + 145 + n, lc:lc + QS].partition_broadcast(128))
                     .then_inc(S["bca"].h[cc % 2], 16)))
            v_bc_all = S["bca"].n[cc % 2]
            xtv = []
            for j in range(NJ):
                jx = cc * NJ + j
                v = S["xt"].inc(jx % 2)
                xtv.append((jx % 2, v))
                add("sync", lambda e, j=j, t0=t0, jx=jx, w=max(0, jx - 1):
                    (e.wait_ge(S["xnsl_"].h, w) if w else None,
                     e.dma_start(out=xT_s[jx % 2][:, :], in_=d_xdslT[j * 128:(j + 1) * 128, t0:t0 + QS])
                     .then_inc(S["xt"].h[jx % 2], 16)))
            vdtp0 = S["dtp"].n
            spbase = S["sp"].n
            wu_war = max(0, 4 * (cc - 1))

            def dtp_j(j):
                if cc == 0 and j < 2:
                    wfree = ("xpe", 24)
                elif cc == 0:
                    wfree = ("sY", j - 1)
                elif j < 2:
                    wfree = ("sY", 4 * (cc - 1) + 3 + j)
                else:
                    wfree = ("sY", 4 * cc + (j - 1))
                wsp = spbase + (2 * (j - 2) + 1) if j >= 2 else 0
                S["dtp"].inc()
                add("tensor", lambda e, j=j, cc=cc, wfree=wfree, wsp=wsp, v_dt=v_dt:
                    (e.wait_ge(S[wfree[0]].h, wfree[1]) if wfree[1] else None,
                     e.wait_ge(S["sp"].h, wsp) if wsp else None,
                     e.wait_ge(S["dtld"].h[cc % 2], v_dt) if j == 0 else None,
                     e.matmul(pxy[j % 2][:, :], wdtT_s[:, j * 128:(j + 1) * 128], dtv[cc % 2][:, :],
                              start=True, stop=True).then_inc(S["dtp"].h, 1)))

            def sp_j(j):
                S["sp"].inc()
                add("scalar", lambda e, j=j, v=vdtp0 + j + 1:
                    (e.wait_ge(S["dtp"].h, v),
                     e.activation(esp[:, :], pxy[j % 2][:, :], AF.Exp, bias=bdt_s[:, j:j + 1])
                     .then_inc(S["sp"].h, 1)))
                S["sp"].inc()
                add("scalar", lambda e, j=j, w=(wu_war + j + 1 if cc > 0 else 0):
                    (e.wait_ge(S["u_"].h, w) if w else None,
                     e.activation(delta[:, j * QS:(j + 1) * QS], esp[:, :], AF.Ln, bias=1.0)
                     .then_inc(S["sp"].h, 1)))

            def emit_j(j):
                jx = cc * NJ + j
                xp_, xv_ = xtv[j]
                add("vector", lambda e, p=xp_, v=xv_: e.wait_ge(S["xt"].h[p], v))
                add("vector", lambda e, p=cc % 2, v=v_mr: e.wait_ge(S["bcb"].h[p], v))
                add("vector", lambda e, jx=jx, cc=cc:
                    e.tensor_tensor(xtmp[:, :], xT_s[jx % 2][:, :], rsB[cc % 2][:, :], ALU.mult))
                S["xnsl_"].inc()
                add("vector", lambda e, j=j, cc=cc:
                    e.tensor_tensor(xnsl[:, j * QS:(j + 1) * QS], xtmp[:, :], mrB[cc % 2][:, :], ALU.subtract)
                    .then_inc(S["xnsl_"].h, 1))
                W("vector", S["sp"], spbase + 2 * j + 2)
                S["u_"].inc()
                add("vector", lambda e, j=j:
                    e.tensor_tensor(u_s[:, j * QS:(j + 1) * QS], delta[:, j * QS:(j + 1) * QS],
                                    xnsl[:, j * QS:(j + 1) * QS], ALU.mult).then_inc(S["u_"].h, 1))
                wyfree = ("sp", spbase + 2 * j + 1)
                wyfree2 = ("sY", 4 * cc + (j - 1)) if j >= 2 else None
                for n in range(NN):
                    g = cc * 64 + j * 16 + n
                    idx = j * NN + n
                    S["ad"].inc()
                    add("scalar", lambda e, j=j, g=g, idx=idx, w=max(0, g - 1):
                        (e.wait_ge(S["scan"].h, w) if w else None,
                         e.activation(ad_s[g % 2][:, :], delta[:, j * QS:(j + 1) * QS], AF.Exp,
                                      scale=A_s[:, idx:idx + 1]).then_inc(S["ad"].h, 1)))
                    add("vector", lambda e, p=cc % 2, v=v_bc_all: e.wait_ge(S["bca"].h[p], v))
                    S["bx"].inc()
                    add("vector", lambda e, j=j, n=n, g=g:
                        e.tensor_tensor(bx_s[g % 2][:, :], u_s[:, j * QS:(j + 1) * QS],
                                        BmA[:, n * QS:(n + 1) * QS], ALU.mult).then_inc(S["bx"].h, 1))
                    S["scan"].inc()
                    add("vector", lambda e, g=g, idx=idx, wad=S["ad"].n, ww=max(0, g - 2):
                        (e.wait_ge(S["ad"].h, wad),
                         e.wait_ge(S["wmul"].h, ww) if ww else None,
                         e.tensor_tensor_scan(h_s[g % 3][:, :], ad_s[g % 2][:, :], bx_s[g % 2][:, :],
                                              hcar[:, idx:idx + 1], ALU.mult, ALU.add)
                         .then_inc(S["scan"].h, 1)))
                    add("vector", lambda e, g=g, idx=idx, wsc=S["scan"].n:
                        (e.wait_ge(S["scan"].h, wsc),
                         e.tensor_copy(hcar[:, idx:idx + 1], h_s[g % 3][:, QS - 1:QS])))
                    S["wmul"].inc()
                    add("gpsimd", lambda e, g=g, n=n, wsc=S["scan"].n, wyi=max(0, g - 1):
                        (e.wait_ge(S["scan"].h, wsc),
                         e.wait_ge(S["yi"].h, wyi) if wyi else None,
                         e.tensor_tensor(w_s[g % 2][:, :], h_s[g % 3][:, :],
                                         CmA[:, n * QS:(n + 1) * QS], ALU.mult).then_inc(S["wmul"].h, 1)))
                    S["yi"].inc()
                    add("tensor", lambda e, j=j, n=n, g=g, ww=S["wmul"].n,
                        wy=(wyfree if n == 0 else None),
                        wy2=(wyfree2 if n == 0 else None):
                        (e.wait_ge(S[wy[0]].h, wy[1]) if wy else None,
                         e.wait_ge(S[wy2[0]].h, wy2[1]) if wy2 else None,
                         e.wait_ge(S["wmul"].h, ww),
                         e.matmul(pxy[j % 2][:, :], ident[:, :], w_s[g % 2][:, :],
                                  start=(n == 0), stop=(n == NN - 1)).then_inc(S["yi"].h, 1)))
                S["sY"].inc()
                wsdma_war = max(0, 16 * (2 * cc + (1 if j >= 2 else 0)))
                add("vector", lambda e, j=j, wy=S["yi"].n, wsd=wsdma_war:
                    (e.wait_ge(S["yi"].h, wy),
                     e.wait_ge(S["sdma"].h[j % 2], wsd) if wsd else None,
                     e.scalar_tensor_tensor(s_st[j % 2][:, :], xnsl[:, j * QS:(j + 1) * QS],
                                            D_s[:, j:j + 1], pxy[j % 2][:, :], ALU.mult, ALU.add)
                     .then_inc(S["sY"].h, 1)))
                q2 = cc // 8
                S["sdma"].inc(j % 2)
                add("sync", lambda e, j=j, q2=q2, lc2=(cc % 8) * QS, ws=S["sY"].n:
                    (e.wait_ge(S["sY"].h, ws),
                     e.dma_start(out=ccs_in[q2].ap()[j * 128:(j + 1) * 128, lc2:lc2 + QS],
                                 in_=s_st[j % 2][:, :]).then_inc(S["sdma"].h[j % 2], 16)))

            dtp_j(0)
            dtp_j(1)
            sp_j(0)
            sp_j(1)
            emit_j(0)
            dtp_j(2)
            sp_j(2)
            emit_j(1)
            if cb:
                cb(0)
            dtp_j(3)
            sp_j(3)
            emit_j(2)
            emit_j(3)
            if cb:
                cb(1)
            if cc % 8 == 7:
                q2 = cc // 8
                S["sgo"].inc()
                add("vector", lambda e, w0=S["sdma"].n[0], w1=S["sdma"].n[1]:
                    (e.wait_ge(S["sdma"].h[0], w0), e.wait_ge(S["sdma"].h[1], w1),
                     e.engine_nop().then_inc(S["sgo"].h, 1)))
                S["ags"].inc()
                add("gpsimd", lambda e, q2=q2, wsg=S["sgo"].n:
                    (e.wait_ge(S["sgo"].h, wsg),
                     e.collective_compute("AllGather", ALU.bypass, replica_groups=GROUPS,
                                          ins=[ccs_in[q2].ap().opt()],
                                          outs=[ccs_out.ap()[q2 * 2048:(q2 + 1) * 2048, :].opt()])
                     .then_inc(S["ags"].h, 1)))

        def make_p3_part(k, p):
            gp = 4 * k + p
            fq = p % 2

            def half(hh):
                for hi in range(8):
                    hb = 16 * p + 8 * hh + hi
                    ghb = HB * k + hb
                    S["wg"].inc(ghb % 2)
                    add("gpsimd", lambda e, hb=hb, ghb=ghb, w=max(0, ghb - 1):
                        (e.wait_ge(S["guu"].h, w) if w else None,
                         e.dma_start(out=wgu_s[ghb % 2][:, :], in_=d_wgu[hb]).then_inc(S["wg"].h[ghb % 2], 16)))
                    vwg = S["wg"].n[ghb % 2]
                    for i in range(KC):
                        last = i == KC - 1
                        if last:
                            S["gug"].inc()
                        add("tensor", lambda e, i=i, ghb=ghb, k=k, last=last, vwg=vwg, wge=ghb:
                            (e.wait_ge(S["wg"].h[ghb % 2], vwg) if i == 0 else None,
                             e.wait_ge(S["xnt"].h, k + 1) if i == 0 else None,
                             e.wait_ge(S["ge"].h, wge) if (i == 0 and wge) else None,
                             e.matmul(pg[:, :], wgu_s[ghb % 2][:, i * 128:(i + 1) * 128],
                                      xnT[k % 2][:, i * QT:(i + 1) * QT], start=(i == 0), stop=last)
                             .then_inc(S["gug"].h, 1) if last else
                             e.matmul(pg[:, :], wgu_s[ghb % 2][:, i * 128:(i + 1) * 128],
                                      xnT[k % 2][:, i * QT:(i + 1) * QT], start=(i == 0), stop=last)))
                    for i in range(KC):
                        last = i == KC - 1
                        if last:
                            S["guu"].inc()
                        add("tensor", lambda e, i=i, ghb=ghb, k=k, last=last, wue=ghb:
                            (e.wait_ge(S["ue"].h, wue) if (i == 0 and wue) else None,
                             e.matmul(pu[:, :], wgu_s[ghb % 2][:, 2048 + i * 128:2176 + i * 128],
                                      xnT[k % 2][:, i * QT:(i + 1) * QT], start=(i == 0), stop=last)
                             .then_inc(S["guu"].h, 1) if last else
                             e.matmul(pu[:, :], wgu_s[ghb % 2][:, 2048 + i * 128:2176 + i * 128],
                                      xnT[k % 2][:, i * QT:(i + 1) * QT], start=(i == 0), stop=last)))
                    S["ge"].inc()
                    add("scalar", lambda e, ghb=ghb, hb=hb, wg=S["gug"].n, wfu=max(0, ghb - 1):
                        (e.wait_ge(S["gug"].h, wg),
                         e.wait_ge(S["fu"].h, wfu) if wfu else None,
                         e.activation(gate_s[ghb % 2][:, :], pg[:, :], AF.Silu, scale=sg_s[:, hb:hb + 1])
                         .then_inc(S["ge"].h, 1)))
                    S["ue"].inc()
                    add("scalar", lambda e, ghb=ghb, hb=hb, wu=S["guu"].n, wfu=max(0, ghb - 1):
                        (e.wait_ge(S["guu"].h, wu),
                         e.wait_ge(S["fu"].h, wfu) if wfu else None,
                         e.mul(up_s[ghb % 2][:, :], pu[:, :], su_s[:, hb:hb + 1]).then_inc(S["ue"].h, 1)))
                    slot = hb % 16
                    wdn = max(0, 16 * (gp - 1))
                    wxp0 = 16 if gp == 0 else 0
                    S["fu"].inc()
                    add("gpsimd", lambda e, ghb=ghb, slot=slot, fq=fq, wge=S["ge"].n, wue=S["ue"].n,
                        wdn=wdn, wxp0=wxp0:
                        (e.wait_ge(S["xp"].h, wxp0) if wxp0 else None,
                         e.wait_ge(S["ge"].h, wge), e.wait_ge(S["ue"].h, wue),
                         e.wait_ge(S["dn"].h, wdn) if wdn else None,
                         e.tensor_tensor(fuse[fq][:, slot * QT:(slot + 1) * QT], gate_s[ghb % 2][:, :],
                                         up_s[ghb % 2][:, :], ALU.mult).then_inc(S["fu"].h, 1)))

            def dn_pass():
                wfu = S["fu"].n
                for db in range(DB):
                    gdb = gp * DB + db
                    S["wdl"].inc(gdb % 2)
                    add("gpsimd", lambda e, db=db, p=p, gdb=gdb, w=max(0, gdb - 1):
                        (e.wait_ge(S["dn"].h, w) if w else None,
                         e.dma_start(out=wd_s[gdb % 2][:, :], in_=d_wd[db][:, p * 2048:(p + 1) * 2048])
                         .then_inc(S["wdl"].h[gdb % 2], 16)))
                    vwd = S["wdl"].n[gdb % 2]
                    for i in range(16):
                        last = i == 15
                        if last:
                            S["dn"].inc()
                        add("tensor", lambda e, i=i, gdb=gdb, fq=fq, last=last, vwd=vwd,
                            wpe=max(0, gdb - 1), wfu=wfu:
                            (e.wait_ge(S["wdl"].h[gdb % 2], vwd) if i == 0 else None,
                             e.wait_ge(S["fu"].h, wfu) if i == 0 else None,
                             e.wait_ge(S["pe_"].h, wpe) if (i == 0 and wpe) else None,
                             e.matmul(pdn[gdb % 2][:, :], wd_s[gdb % 2][:, i * 128:(i + 1) * 128],
                                      fuse[fq][:, i * QT:(i + 1) * QT], start=(i == 0), stop=last)
                             .then_inc(S["dn"].h, 1) if last else
                             e.matmul(pdn[gdb % 2][:, :], wd_s[gdb % 2][:, i * 128:(i + 1) * 128],
                                      fuse[fq][:, i * QT:(i + 1) * QT], start=(i == 0), stop=last)))
                    vdn = S["dn"].n
                    if p == 0:
                        wst = S["pst"].n[gdb % 2]
                        wld = S["pld"].n[gdb % 2]
                        S["pe_"].inc()
                        add("vector", lambda e, gdb=gdb, vdn=vdn, wst=wst:
                            (e.wait_ge(S["dn"].h, vdn),
                             e.wait_ge(S["pst"].h[gdb % 2], wst) if wst else None,
                             e.tensor_copy(pstg[gdb % 2][:, :], pdn[gdb % 2][:, :]).then_inc(S["pe_"].h, 1)))
                        S["pst"].inc(gdb % 2)
                        pst_after[gdb] = S["pst"].n[gdb % 2]
                        add("sync", lambda e, db=db, gdb=gdb, wpe=S["pe_"].n, wld=wld:
                            (e.wait_ge(S["pe_"].h, wpe),
                             e.wait_ge(S["pld"].h[gdb % 2], wld) if wld else None,
                             e.dma_start(out=d_phbm.ap()[db * 128:(db + 1) * 128, :], in_=pstg[gdb % 2][:, :])
                             .then_inc(S["pst"].h[gdb % 2], 16)))
                    else:
                        wst = S["pst"].n[gdb % 2]
                        S["pld"].inc(gdb % 2)
                        pld_after[gdb] = S["pld"].n[gdb % 2]
                        add("sync", lambda e, db=db, gdb=gdb, wst=wst, wpe=max(0, gdb - 1):
                            (e.wait_ge(S["pst"].h[gdb % 2], wst),
                             e.wait_ge(S["pe_"].h, wpe) if wpe else None,
                             e.dma_start(out=pldg[gdb % 2][:, :], in_=d_phbm.ap()[db * 128:(db + 1) * 128, :])
                             .then_inc(S["pld"].h[gdb % 2], 16)))
                        if p < 3:
                            wst2 = S["pst"].n[gdb % 2]
                            S["pe_"].inc()
                            add("vector", lambda e, gdb=gdb, vdn=vdn, wld=S["pld"].n[gdb % 2], wst=wst2:
                                (e.wait_ge(S["dn"].h, vdn), e.wait_ge(S["pld"].h[gdb % 2], wld),
                                 e.wait_ge(S["pst"].h[gdb % 2], wst) if wst else None,
                                 e.tensor_tensor(pstg[gdb % 2][:, :], pldg[gdb % 2][:, :], pdn[gdb % 2][:, :],
                                                 ALU.add).then_inc(S["pe_"].h, 1)))
                            S["pst"].inc(gdb % 2)
                            pst_after[gdb] = S["pst"].n[gdb % 2]
                            add("sync", lambda e, db=db, gdb=gdb, wpe=S["pe_"].n:
                                (e.wait_ge(S["pe_"].h, wpe),
                                 e.dma_start(out=d_phbm.ap()[db * 128:(db + 1) * 128, :], in_=pstg[gdb % 2][:, :])
                                 .then_inc(S["pst"].h[gdb % 2], 16)))
                        else:
                            S["pe_"].inc()
                            add("vector", lambda e, gdb=gdb, db=db, vdn=vdn, wld=S["pld"].n[gdb % 2],
                                wgs=max(0, k * DB + db - 1):
                                (e.wait_ge(S["dn"].h, vdn), e.wait_ge(S["pld"].h[gdb % 2], wld),
                                 e.wait_ge(S["gsil"].h, wgs) if wgs else None,
                                 e.tensor_tensor(gpre[db % 2][:, :], pldg[gdb % 2][:, :], pdn[gdb % 2][:, :],
                                                 ALU.add).then_inc(S["pe_"].h, 1)))
                            S["gsil"].inc()
                            wgd = S["gdma"].n[db % 2]
                            add("scalar", lambda e, db=db, wpe=S["pe_"].n, wgd=wgd:
                                (e.wait_ge(S["pe_"].h, wpe),
                                 e.wait_ge(S["gdma"].h[db % 2], wgd) if wgd else None,
                                 e.activation(g_st[db % 2][:, :], gpre[db % 2][:, :], AF.Silu,
                                              scale=sd_s[:, db:db + 1]).then_inc(S["gsil"].h, 1)))
                            S["gdma"].inc(db % 2)
                            gdma_after[k * DB + db] = S["gdma"].n[db % 2]
                            add("sync", lambda e, k=k, db=db, wg=S["gsil"].n:
                                (e.wait_ge(S["gsil"].h, wg),
                                 e.dma_start(out=d_ghbm.ap()[(k * DB + db) * 128:(k * DB + db + 1) * 128, :],
                                             in_=g_st[db % 2][:, :]).then_inc(S["gdma"].h[db % 2], 16)))

            return half, dn_pass

        def emit_p4():
            P4ONLY = os.environ.get("K_P4ONLY") == "1"
            if not P4ONLY:
                S["ags"].inc()
                add("gpsimd", lambda e, w=S["ags"].n - 1, wl=S["ldc"].n:
                    (e.wait_ge(S["ags"].h, w), e.wait_ge(S["ldc"].h, wl),
                     e.collective_compute("AllReduce", ALU.add, replica_groups=GROUPS,
                                          ins=[bar_in.ap().opt()], outs=[bar_out.ap().opt()])
                     .then_inc(S["ags"].h, 1)))
            for part in range(4):
                S["smv"].inc(16)
                add("gpsimd", lambda e, part=part, w=(0 if P4ONLY else S["ags"].n):
                    (e.wait_ge(S["ags"].h, w) if (w and part == 0) else None,
                     e.dma_start(out=s_mine.ap()[part * 512:(part + 1) * 512, :],
                                 in_=ccs_out.ap()[bass.ds((e.partition_id() % 4) * 2048 + part * 512, 512), :])
                     .then_inc(S["smv"].h, 16)))
            W("sync", S["smv"], S["smv"].n)
            if not P4ONLY:
                W("vector", S["ags"], 4)
            P4N = int(os.environ.get("K_P4N", str(NT * DB)))
            for k in range(NT):
                for db in range(DB):
                    if k * DB + db >= P4N:
                        continue
                    i = (k * DB + db) % 2
                    kd = k * DB + db
                    S["sld"].inc(i)
                    add("sync", lambda e, k=k, db=db, i=i, wsg=max(0, kd - 1):
                        (e.wait_ge(S["sg_"].h, wsg) if wsg else None,
                         e.dma_start(out=s_ld[i][:, :],
                                     in_=s_mine.ap()[(db // 4) * 512 + (db % 4) * 128:(db // 4) * 512 + (db % 4) * 128 + 128,
                                                     k * QT:(k + 1) * QT]).then_inc(S["sld"].h[i], 16)))
                    S["gld"].inc(i)
                    add("sync", lambda e, k=k, db=db, i=i, kd=kd,
                        wg=(0 if P4ONLY else S["gdma"].n[i]), wsg=max(0, kd - 1):
                        (e.wait_ge(S["gdma"].h[i], wg) if wg else None,
                         e.wait_ge(S["sg_"].h, wsg) if wsg else None,
                         e.dma_start(out=g_ld[i][:, :],
                                     in_=d_ghbm.ap()[kd * 128:(kd + 1) * 128, :]).then_inc(S["gld"].h[i], 16)))
                    S["xtld"].inc(i)
                    add("sync", lambda e, k=k, db=db, i=i, wof=max(0, kd - 1):
                        (e.wait_ge(S["ofa"].h, wof) if wof else None,
                         e.dma_start(out=xT_ld[i][:, :],
                                     in_=d_xtokT[db * 128:(db + 1) * 128, k * QT:(k + 1) * QT])
                         .then_inc(S["xtld"].h[i], 16)))
                    S["sg_"].inc()
                    add("vector", lambda e, i=i, ws=S["sld"].n[i], wg=S["gld"].n[i]:
                        (e.wait_ge(S["sld"].h[i], ws), e.wait_ge(S["gld"].h[i], wg),
                         e.tensor_tensor(sg4[i][:, :], s_ld[i][:, :], g_ld[i][:, :], ALU.mult)
                         .then_inc(S["sg_"].h, 1)))
                    S["ofa"].inc()
                    wo = out_after.get(kd - 2, 0)
                    add("vector", lambda e, i=i, wx=S["xtld"].n[i], wo=wo:
                        (e.wait_ge(S["xtld"].h[i], wx),
                         e.wait_ge(S["out"].h[i], wo) if wo else None,
                         e.tensor_tensor(of_s[i][:, :], sg4[i][:, :], xT_ld[i][:, :], ALU.add)
                         .then_inc(S["ofa"].h, 1)))
                    S["out"].inc(i)
                    out_after[kd] = S["out"].n[i]
                    add("sync", lambda e, k=k, db=db, i=i, wo=S["ofa"].n:
                        (e.wait_ge(S["ofa"].h, wo),
                         e.dma_start(out=d_out[db * 128:(db + 1) * 128, k * QT:(k + 1) * QT],
                                     in_=of_s[i][:, :]).then_inc(S["out"].h[i], 16)))
            add("sync", lambda e, v0=S["out"].n[0], v1=S["out"].n[1]:
                (e.wait_ge(S["out"].h[0], v0), e.wait_ge(S["out"].h[1], v1)))
            if DBGOUT == "post2":
                S["out"].inc(0)
                add("gpsimd", lambda e: e.dma_start(out=d_dbg, in_=s_mine.ap()).then_inc(S["out"].h[0], 16))
                add("sync", lambda e, v=S["out"].n[0]: e.wait_ge(S["out"].h[0], v))
            if DBGOUT == "post":
                S["out"].inc(0)
                add("gpsimd", lambda e: e.dma_start(out=d_dbg[0:1024, :],
                                                    in_=ccs_out.ap()[0:1024, :]).then_inc(S["out"].h[0], 16))
                S["out"].inc(0)
                add("gpsimd", lambda e: e.dma_start(out=d_dbg[1024:2048, 0:512],
                                                    in_=d_ghbm.ap()[0:1024, :]).then_inc(S["out"].h[0], 16))
                add("sync", lambda e, v=S["out"].n[0]: e.wait_ge(S["out"].h[0], v))

        NTILES = int(os.environ.get("K_NTILES", str(NT)))
        NOAG = os.environ.get("K_NOAG") == "1"
        NOSTATS = os.environ.get("K_NOSTATS") == "1"
        NOP1B = os.environ.get("K_NOP1B") == "1"
        NOXPROJ = os.environ.get("K_NOXPROJ") == "1"
        emit_consts()
        if os.environ.get("K_P4ONLY") == "1":
            emit_p4()
            return finish(nc, ev)
        for k in range(NTILES):
            emit_p1a_tile(k)
        if not NOSTATS:
            sv = S["stat_"].inc()
            add("vector", lambda e: e.tensor_copy(stats_bf[:, :], stats_all[:, :]).then_inc(S["stat_"].h, 1))
            t = tr_cnt[0]
            tr_cnt[0] += 1
            tv = S["tr"].inc(16)
            add("tensor", lambda e, t=t, sv=sv:
                (e.wait_ge(S["stat_"].h, sv),
                 e.wait_ge(S["trv"].h, t - 1) if t >= 1 else None,
                 e.transpose(ptr[t % 2][0:64, 0:128], stats_bf[:, :], ident[:, :]).then_inc(S["tr"].h, 16)))
            S["trv"].inc()
            sv2 = S["stat_"].inc()
            add("vector", lambda e, t=t, tv=tv:
                (e.wait_ge(S["tr"].h, tv),
                 e.tensor_copy(stT_sb[:, :], ptr[t % 2][0:64, 0:128]).then_inc(S["trv"].h, 1)))
            add("vector", lambda e: e.engine_nop().then_inc(S["stat_"].h, 1))
            for rowblk, srcsl in ((160, (0, 32)), (161, (32, 64))):
                S["ctr"].inc(0)
                add("sync", lambda e, rowblk=rowblk, srcsl=srcsl, sv2=sv2:
                    (e.wait_ge(S["stat_"].h, sv2),
                     e.dma_start(out=cc1_in.ap()[rowblk:rowblk + 1, :].rearrange("o (b f) -> (o b) f", b=32),
                                 in_=stT_sb[srcsl[0]:srcsl[1], :]).then_inc(S["ctr"].h[0], 16)))
        if not NOAG:
            emit_ag1()
        if not NOP1B:
            emit_p1b_tile(0)
            emit_p1b_tile(1)
        STAGE = int(os.environ.get("K_STAGE", "9"))
        if STAGE <= 1:
            if not NOAG:
                add("sync", lambda e: e.wait_ge(S["ag1"].h, 1))
            add("vector", lambda e, v=S["xnt"].n: e.wait_ge(S["xnt"].h, v))
            add("vector", lambda e, v=S["trv"].n: e.wait_ge(S["trv"].h, v))
            add("sync", lambda e, v0=S["ctr"].n[0], v1=S["ctr"].n[1]:
                (e.wait_ge(S["ctr"].h[0], v0), e.wait_ge(S["ctr"].h[1], v1) if v1 else None))
            add("scalar", lambda e, v=S["rstd"].n: e.wait_ge(S["rstd"].h, v))
            if DBGOUT == "cc1i":
                S["out"].inc(0)
                add("sync", lambda e: e.dma_start(out=d_dbg[0:162, :], in_=cc1_in.ap()).then_inc(S["out"].h[0], 16))
                add("sync", lambda e: e.wait_ge(S["out"].h[0], 16))
            elif DBGOUT == "stats":
                S["out"].inc(0)
                add("gpsimd", lambda e: e.dma_start(out=d_dbg[0:128, 0:64], in_=stats_all[:, :]).then_inc(S["out"].h[0], 16))
                add("sync", lambda e: e.wait_ge(S["out"].h[0], 16))
            elif DBGOUT == "xnt":
                S["out"].inc(0)
                add("sync", lambda e: e.dma_start(out=d_dbg[0:128, :],
                                                  in_=fuse[0][:, 0:4096]).then_inc(S["out"].h[0], 16))
                S["out"].inc(0)
                add("sync", lambda e: e.dma_start(out=d_dbg[128:256, :],
                                                  in_=fuse[0][:, 4096:8192]).then_inc(S["out"].h[0], 16))
                add("sync", lambda e: e.wait_ge(S["out"].h[0], 32))
            return finish(nc, ev)
        NSLOTS = int(os.environ.get("K_NSLOTS", str(32 + OFFS)))
        NOP2 = os.environ.get("K_NOP2") == "1"
        NOP3 = os.environ.get("K_NOP3") == "1"
        for s in range(min(32 + OFFS, NSLOTS)):
            part = (divmod(s, 4) if s < 32 else None) if not NOP3 else None
            cc = s - OFFS
            halff = dnf = None
            if part:
                halff, dnf = make_p3_part(*part)
            if 0 <= cc < NCH and not NOP2:
                emit_p2_chunk(cc, cb=halff)
                if dnf:
                    dnf()
            elif part:
                halff(0)
                halff(1)
                dnf()
            if part and part[1] == 3 and part[0] + 2 < NT and not NOP2 is False or (part and part[1] == 3 and part[0] + 2 < NT):
                emit_p1b_tile(part[0] + 2)
        if STAGE <= 2:
            add("sync", lambda e, v0=S["gdma"].n[0], v1=S["gdma"].n[1]:
                (e.wait_ge(S["gdma"].h[0], v0) if v0 else None,
                 e.wait_ge(S["gdma"].h[1], v1) if v1 else None))
            add("sync", lambda e, v=S["ags"].n: e.wait_ge(S["ags"].h, v))
            if DBGOUT == "cc1":
                S["out"].inc(0)
                add("sync", lambda e: e.dma_start(out=d_dbg[0:648, :], in_=cc1_out.ap()).then_inc(S["out"].h[0], 16))
                add("sync", lambda e: e.wait_ge(S["out"].h[0], 16))
            elif DBGOUT == "s":
                S["out"].inc(0)
                add("sync", lambda e: e.dma_start(out=d_dbg, in_=ccs_out.ap()[0:2048, :]).then_inc(S["out"].h[0], 16))
                add("sync", lambda e: e.wait_ge(S["out"].h[0], 16))
            elif DBGOUT == "g7":
                S["out"].inc(0)
                add("sync", lambda e: e.dma_start(out=d_dbg[0:2048, 0:512],
                                                  in_=d_ghbm.ap()[7 * 2048:8 * 2048, :]).then_inc(S["out"].h[0], 16))
                add("sync", lambda e: e.wait_ge(S["out"].h[0], 16))
            elif DBGOUT == "sm":
                S["out"].inc(0)
                add("gpsimd", lambda e: e.dma_start(out=d_dbg, in_=ccs_out.ap()[0:2048, :]).then_inc(S["out"].h[0], 16))
                add("sync", lambda e: e.wait_ge(S["out"].h[0], 16))
            elif DBGOUT == "g":
                S["out"].inc(0)
                add("sync", lambda e: e.dma_start(out=d_dbg[0:2048, 0:512],
                                                  in_=d_ghbm.ap()[0:2048, :]).then_inc(S["out"].h[0], 16))
                add("sync", lambda e: e.wait_ge(S["out"].h[0], 16))
            return finish(nc, ev)
        emit_p4()
        return finish(nc, ev)

    return nc


def kernel(L0_attn_x, L0_attn_W_xproj, L0_attn_W_dt, L0_attn_b_dt, L0_attn_A, L0_attn_D,
           L0_ffn_w_gate, L0_ffn_w_up, L0_ffn_w_down, L0_ffn_s_gate, L0_ffn_s_up, L0_ffn_s_down):
    x = np.asarray(L0_attn_x, np.float32)
    Wx = np.asarray(L0_attn_W_xproj, np.float32)
    Wdt = np.asarray(L0_attn_W_dt, np.float32)
    bdt = np.asarray(L0_attn_b_dt, np.float32)
    A = np.asarray(L0_attn_A, np.float32)
    Dv = np.asarray(L0_attn_D, np.float32)
    wg = np.asarray(L0_ffn_w_gate, np.int8)
    wu = np.asarray(L0_ffn_w_up, np.int8)
    wdn = np.asarray(L0_ffn_w_down, np.int8)
    sgv = np.asarray(L0_ffn_s_gate, np.float32)
    suv = np.asarray(L0_ffn_s_up, np.float32)
    sdv = np.asarray(L0_ffn_s_down, np.float32)

    wg_t = wg.reshape(KC, 128, HB, 128).transpose(2, 1, 0, 3).reshape(HB, 128, 2048)
    wu_t = wu.reshape(KC, 128, HB, 128).transpose(2, 1, 0, 3).reshape(HB, 128, 2048)
    wgu_np = np.ascontiguousarray(np.concatenate([wg_t, wu_t], axis=2))
    wd_np = np.ascontiguousarray(wdn.reshape(HB, 128, DB, 128).transpose(2, 1, 0, 3).reshape(DB, 128, HH))
    WxT = np.ascontiguousarray(Wx.T)
    wxdt_np = np.ascontiguousarray(WxT[:, 0:128].reshape(KC, 128, 128))
    wxbm_np = np.ascontiguousarray(WxT[:, 128:144].reshape(KC, 128, 16))
    wxcm_np = np.ascontiguousarray(WxT[:, 144:160].reshape(KC, 128, 16))
    sg_r = np.ascontiguousarray(sgv.reshape(HB, 128).T)
    su_r = np.ascontiguousarray(suv.reshape(HB, 128).T)
    sd_r = np.ascontiguousarray(sdv.reshape(DB, 128).T)
    ident_np = np.eye(128, dtype=BF)

    in_maps = []
    for c in range(8):
        b, q = divmod(c, 4)
        dsl = slice(512 * q, 512 * (q + 1))
        in_maps.append({
            "x_tok": np.ascontiguousarray(x[b, q * TOK:(q + 1) * TOK, :]),
            "x_tokT": np.ascontiguousarray(x[b, q * TOK:(q + 1) * TOK, :].T),
            "x_dslT": np.ascontiguousarray(x[b, :, dsl].T),
            "wgu": wgu_np, "wd": wd_np,
            "wxdt": wxdt_np, "wxbm": wxbm_np, "wxcm": wxcm_np,
            "wdtT": np.ascontiguousarray(Wdt[dsl, :].T),
            "A_r": np.ascontiguousarray(A[dsl].reshape(NJ, 128, NN).transpose(1, 0, 2).reshape(128, NJ * NN)),
            "bdt_r": np.ascontiguousarray(bdt[dsl].reshape(NJ, 128).T),
            "D_r": np.ascontiguousarray(Dv[dsl].reshape(NJ, 128).T),
            "sg_r": sg_r, "su_r": su_r, "sd_r": sd_r,
            "ident": ident_np, "eps_c": np.full((128, 1), EPS, np.float32),
        })

    nc = build_program()
    res = run_bass_kernel_spmd(nc, in_maps, core_ids=list(range(8)))
    out = np.empty((BB, LL, DD), np.float32)
    for c in range(8):
        b, q = divmod(c, 4)
        out[b, q * TOK:(q + 1) * TOK, :] = res.results[c]["outT"].T
    return out


# version 07
# speedup vs baseline: 1.4026x; 1.4026x over previous
import os
import numpy as np
import ml_dtypes
import concourse.bass as bass
import concourse.mybir as mybir
from concourse.bass_utils import run_bass_kernel_spmd

F32 = mybir.dt.float32
BF16 = mybir.dt.bfloat16
I8 = mybir.dt.int8
AF = mybir.ActivationFunctionType
ALU = mybir.AluOpType
AXX = mybir.AxisListType.X
BF = ml_dtypes.bfloat16

BB, LL, DD, NN, RK, HH = 2, 16384, 2048, 16, 128, 8192
TOK = 4096
QT = 512
NT = TOK // QT
QS = 512
NCH = LL // QS
NJ = 4
HB = 64
KC = 16
DB = 16
GROUPS = [[0, 1, 2, 3], [4, 5, 6, 7]]
CCROWS = 162
EPS = 1e-5
OFFS = 4


class Sem:
    def __init__(self, h):
        self.h = h
        self.n = 0

    def inc(self, by=1):
        self.n += by
        return self.n


class PS:

    def __init__(self, h0, h1):
        self.h = (h0, h1)
        self.n = [0, 0]

    def inc(self, p):
        self.n[p] += 16
        return self.n[p]


def finish(nc, ev):
    with nc.Block() as blk:
        @blk.sync
        def _(sync):
            for fn in ev["sync"]:
                fn(sync)

        @blk.tensor
        def _(tensor):
            for fn in ev["tensor"]:
                fn(tensor)

        @blk.vector
        def _(vector):
            for fn in ev["vector"]:
                fn(vector)

        @blk.scalar
        def _(scalar):
            for fn in ev["scalar"]:
                fn(scalar)

        @blk.gpsimd
        def _(gpsimd):
            for fn in ev["gpsimd"]:
                fn(gpsimd)
    return nc


def build_program():
    nc = bass.Bass("TRN2", target_bir_lowering=False, debug=False,
                   detect_race_conditions=False)

    d_xtok = nc.dram_tensor("x_tok", [TOK, DD], F32, kind="ExternalInput").ap()
    d_xtokT = nc.dram_tensor("x_tokT", [DD, TOK], F32, kind="ExternalInput").ap()
    d_xdslT = nc.dram_tensor("x_dslT", [NJ * 128, LL], F32, kind="ExternalInput").ap()
    d_wgu = nc.dram_tensor("wgu", [HB, 128, 4096], I8, kind="ExternalInput").ap()
    d_wd = nc.dram_tensor("wd", [DB, 128, HH], I8, kind="ExternalInput").ap()
    d_wxdt = nc.dram_tensor("wxdt", [KC, 128, 128], F32, kind="ExternalInput").ap()
    d_wxbm = nc.dram_tensor("wxbm", [KC, 128, 16], F32, kind="ExternalInput").ap()
    d_wxcm = nc.dram_tensor("wxcm", [KC, 128, 16], F32, kind="ExternalInput").ap()
    d_wdtT = nc.dram_tensor("wdtT", [128, NJ * 128], F32, kind="ExternalInput").ap()
    d_A = nc.dram_tensor("A_r", [128, NJ * NN], F32, kind="ExternalInput").ap()
    d_bdt = nc.dram_tensor("bdt_r", [128, NJ], F32, kind="ExternalInput").ap()
    d_D = nc.dram_tensor("D_r", [128, NJ], F32, kind="ExternalInput").ap()
    d_sg = nc.dram_tensor("sg_r", [128, HB], F32, kind="ExternalInput").ap()
    d_su = nc.dram_tensor("su_r", [128, HB], F32, kind="ExternalInput").ap()
    d_sd = nc.dram_tensor("sd_r", [128, DB], F32, kind="ExternalInput").ap()
    d_id = nc.dram_tensor("ident", [128, 128], BF16, kind="ExternalInput").ap()
    d_eps = nc.dram_tensor("eps_c", [128, 1], F32, kind="ExternalInput").ap()
    d_out = nc.dram_tensor("outT", [DD, TOK], F32, kind="ExternalOutput").ap()
    DBGOUT = os.environ.get("K_DBGOUT", "")
    d_dbg = (nc.dram_tensor("dbg", [2048, 4096], BF16, kind="ExternalOutput").ap()
             if DBGOUT else None)

    cc1_in = nc.dram_tensor("cc1_in", [CCROWS, TOK], BF16)
    cc1_out = nc.dram_tensor("cc1_out", [4 * CCROWS, TOK], BF16)
    ccs_in = [nc.dram_tensor(f"ccs_in{q}", [NJ * 128, 4096], BF16) for q in range(4)]
    ccs_out = nc.dram_tensor("ccs_out", [4 * 2048, 4096], BF16)
    d_ghbm = nc.dram_tensor("g_hbm", [NT * DB * 128, QT], BF16)
    bar_in = nc.dram_tensor("bar_in", [128, 1], F32)
    bar_out = nc.dram_tensor("bar_out", [128, 1], F32)
    s_mine = nc.dram_tensor("s_mine", [2048, 4096], BF16)
    d_phbm = nc.dram_tensor("p_hbm", [DB * 128, QT], BF16)

    import contextlib
    es = contextlib.ExitStack()
    sbt = lambda n, s, d: es.enter_context(nc.sbuf_tensor(n, s, d))
    psm = lambda n, s, d: es.enter_context(nc.psum_tensor(n, s, d))

    with es:
        ident = sbt("ident_s", [128, 128], BF16)
        wxdt_s = sbt("wxdt_s", [128, KC * 128], BF16)
        wxbm_s = sbt("wxbm_s", [128, KC * 16], BF16)
        wxcm_s = sbt("wxcm_s", [128, KC * 16], BF16)
        wdtT_s = sbt("wdtT_s", [128, NJ * 128], BF16)
        A_s = sbt("A_s", [128, NJ * NN], F32)
        bdt_s = sbt("bdt_s", [128, NJ], F32)
        D_s = sbt("D_s", [128, NJ], F32)
        sg_s = sbt("sg_s", [128, HB], F32)
        su_s = sbt("su_s", [128, HB], F32)
        sd_s = sbt("sd_s", [128, DB], F32)
        eps_s = sbt("eps_s", [128, 1], F32)
        x_c = sbt("x_c", [128, DD], F32)
        xsq = sbt("xsq", [128, DD], BF16)
        xn_tm = sbt("xn_tm", [128, DD], BF16)
        stat = sbt("stat", [128, 8], F32)
        stats_all = sbt("stats_all", [128, 64], F32)
        stats_bf = sbt("stats_bf", [128, 64], BF16)
        stT_sb = sbt("stT_sb", [64, 128], BF16)
        xnT = [sbt(f"xnT{i}", [128, KC * QT], BF16) for i in range(2)]
        dt_ev = [sbt(f"dt_ev{i}", [128, QT], BF16) for i in range(2)]
        bc_ev = [sbt(f"bc_ev{i}", [48, QT], BF16) for i in range(2)]
        xT_s = [sbt(f"xT{i}", [128, QS], F32) for i in range(2)]
        xtmp = sbt("xtmp", [128, QS], F32)
        mrB = [sbt(f"mrB{i}", [128, QS], BF16) for i in range(2)]
        rsB = [sbt(f"rsB{i}", [128, QS], BF16) for i in range(2)]
        dtv = [sbt(f"dtv{i}", [128, QS], BF16) for i in range(2)]
        esp = sbt("esp", [128, QS], F32)
        delta = sbt("delta", [128, NJ * QS], F32)
        xnsl = sbt("xnsl", [128, NJ * QS], BF16)
        u_s = sbt("u_s", [128, NJ * QS], BF16)
        BmA = sbt("BmA", [128, NN * QS], BF16)
        CmA = sbt("CmA", [128, NN * QS], BF16)
        ad_s = [sbt(f"ad{i}", [128, QS], F32) for i in range(2)]
        bx_s = [sbt(f"bx{i}", [128, QS], BF16) for i in range(2)]
        h_s = [sbt(f"h{i}", [128, QS], BF16) for i in range(3)]
        w_s = [sbt(f"w{i}", [128, QS], BF16) for i in range(2)]
        hcar = sbt("hcar", [128, NJ * NN], F32)
        s_st = [sbt(f"s_st{i}", [128, QS], BF16) for i in range(2)]
        wgu_s = [sbt(f"wgu{i}", [128, 4096], BF16) for i in range(2)]
        wd_s = [sbt(f"wd{i}", [128, 2048], BF16) for i in range(2)]
        gate_s = [sbt(f"gate{i}", [128, QT], BF16) for i in range(2)]
        up_s = [sbt(f"up{i}", [128, QT], BF16) for i in range(2)]
        fuse = [sbt(f"fuse{i}", [128, 16 * QT], BF16) for i in range(2)]
        pstg = [sbt(f"pstg{i}", [128, QT], BF16) for i in range(2)]
        pldg = [sbt(f"pldg{i}", [128, QT], BF16) for i in range(2)]
        gpre = [sbt(f"gpre{i}", [128, QT], F32) for i in range(2)]
        g_st = [sbt(f"g_st{i}", [128, QT], BF16) for i in range(2)]
        s_ld = bx_s
        g_ld = h_s[:2]
        xT_ld = xT_s
        sg4 = w_s
        of_s = ad_s

        pg = psm("pg", [128, QT], F32)
        pu = psm("pu", [128, QT], F32)
        pdn = [psm(f"pdn{i}", [128, QT], F32) for i in range(2)]
        pxy = [psm(f"pxy{i}", [128, QS], F32) for i in range(2)]
        ptr = [psm(f"ptr{i}", [128, 1024], BF16) for i in range(2)]

        semnames = [
            "ldc", "xld", "sq", "stat_", "rstd", "xn", "tr", "trv", "xnt",
            "xp", "xpe", "ag1",
            "dtp", "sp", "xnsl_", "u_",
            "ad", "bx", "scan", "wmul", "yi", "sY", "ags",
            "gug", "guu", "ge", "ue", "fu", "dn", "pe_",
            "gsil",
            "sg_", "ofa", "smv", "ldcg", "vch", "sch", "sgo",
        ]
        S = {nm: Sem(es.enter_context(nc.semaphore(nm))) for nm in semnames}
        psnames = ["ctr", "xt", "dtld", "bcb", "bca", "sdma",
                   "wg", "wdl", "pst", "pld", "gdma",
                   "sld", "gld", "xtld", "out"]
        for nm in psnames:
            S[nm] = PS(es.enter_context(nc.semaphore(nm + "0")),
                       es.enter_context(nc.semaphore(nm + "1")))
        ev = {e: [] for e in ("sync", "tensor", "vector", "scalar", "gpsimd")}

        def add(eng, fn):
            ev[eng].append(fn)

        def W(eng, sem, val):
            if val > 0:
                add(eng, lambda e, s=sem.h, v=val: e.wait_ge(s, v))

        tr_cnt = [0]
        ctr_after_tile = {}
        pst_after = {}
        gdma_after = {}
        out_after = {}
        pld_after = {}
        P4ST = {}

        def emit_consts():
            for dst, src in ((A_s, d_A), (bdt_s, d_bdt), (D_s, d_D),
                             (sg_s, d_sg), (su_s, d_su), (sd_s, d_sd)):
                S["ldc"].inc(16)
                add("sync", lambda e, d=dst, s=src: e.dma_start(out=d[:, :], in_=s).then_inc(S["ldc"].h, 16))
            S["ldc"].inc(16)
            add("sync", lambda e: e.dma_start(out=ident[:, :], in_=d_id).then_inc(S["ldc"].h, 16))
            S["ldc"].inc(16)
            add("sync", lambda e: e.dma_start(out=eps_s[:, :], in_=d_eps).then_inc(S["ldc"].h, 16))
            weps = S["ldc"].n
            S["ldc"].inc(16)
            add("sync", lambda e, w=weps:
                (e.wait_ge(S["ldc"].h, w),
                 e.dma_start(out=bar_in.ap()[:, :], in_=eps_s[:, 0:1]).then_inc(S["ldc"].h, 16)))
            S["ldcg"].inc(16)
            add("gpsimd", lambda e: e.dma_start(out=wdtT_s[:, :], in_=d_wdtT).then_inc(S["ldcg"].h, 16))
            for dst, src, f in ((wxdt_s, d_wxdt, 128), (wxbm_s, d_wxbm, 16), (wxcm_s, d_wxcm, 16)):
                for i in range(KC):
                    S["ldcg"].inc(16)
                    add("gpsimd", lambda e, d=dst, s=src, i=i, f=f:
                        e.dma_start(out=d[:, i * f:(i + 1) * f], in_=s[i]).then_inc(S["ldcg"].h, 16))
            add("vector", lambda e: e.memset(hcar[:, :], 0.0))
            for eng in ("tensor", "vector", "scalar"):
                W(eng, S["ldc"], S["ldc"].n)
                W(eng, S["ldcg"], S["ldcg"].n)

        def emit_ln_chunk(k, ch, xnT_dst, p1a, last_of_tile=False):
            col = k * 4 + ch
            wv_sq, wv_xn = S["sq"].n, S["xn"].n
            v = S["xld"].inc(16)
            add("sync", lambda e, col=col, wsq=wv_sq, wxn=wv_xn:
                (e.wait_ge(S["sq"].h, wsq) if wsq else None,
                 e.wait_ge(S["xn"].h, wxn) if wxn else None,
                 e.dma_start(out=x_c[:, :], in_=d_xtok[col * 128:(col + 1) * 128, :])
                 .then_inc(S["xld"].h, 16)))
            wtr = 16 * tr_cnt[0]
            if p1a:
                W("scalar", S["xld"], v)
                sqv = S["sq"].inc()
                add("scalar", lambda e: e.activation(xsq[:, :], x_c[:, :], AF.Square).then_inc(S["sq"].h, 1))
                W("vector", S["xld"], v)
                W("vector", S["sq"], sqv)

                def vc(fn):
                    w = S["vch"].n
                    S["vch"].inc()
                    add("vector", lambda e, w=w, fn=fn:
                        (e.wait_ge(S["vch"].h, w) if w else None, fn(e).then_inc(S["vch"].h, 1)))

                vc(lambda e: e.reduce_sum(stat[:, 0:1], x_c[:, :], AXX))
                vc(lambda e: e.reduce_sum(stat[:, 1:2], xsq[:, :], AXX))
                vc(lambda e: e.tensor_scalar_mul(stat[:, 2:3], stat[:, 0:1], 1.0 / DD))
                vc(lambda e: e.tensor_scalar_mul(stat[:, 3:4], stat[:, 1:2], 1.0 / DD))
                vc(lambda e: e.tensor_tensor(stat[:, 0:1], stat[:, 2:3], stat[:, 2:3], ALU.mult))
                vc(lambda e: e.tensor_tensor(stat[:, 3:4], stat[:, 3:4], stat[:, 0:1], ALU.subtract))
                W("scalar", S["vch"], S["vch"].n)
                wsc = S["sch"].n
                S["sch"].inc()
                add("scalar", lambda e, w=wsc: (e.wait_ge(S["sch"].h, w) if w else None,
                                                e.activation(stat[:, 4:5], stat[:, 3:4], AF.Ln, bias=eps_s[:, 0:1])
                                                .then_inc(S["sch"].h, 1)))
                wsc = S["sch"].n
                S["sch"].inc()
                add("scalar", lambda e, w=wsc: (e.wait_ge(S["sch"].h, w),
                                                e.activation(stat[:, 4:5], stat[:, 4:5], AF.Exp, scale=-0.5)
                                                .then_inc(S["sch"].h, 1)))
                W("vector", S["sch"], S["sch"].n)
                W("vector", S["tr"], wtr)
                W("vector", S["vch"], S["vch"].n)
                xnv = S["xn"].inc()
                add("vector", lambda e: e.tensor_scalar(xn_tm[:, :], x_c[:, :], stat[:, 2:3], stat[:, 4:5],
                                                        ALU.subtract, ALU.mult).then_inc(S["xn"].h, 1))
                vc(lambda e: e.tensor_tensor(stat[:, 5:6], stat[:, 2:3], stat[:, 4:5], ALU.mult))
                vc(lambda e, col=col: e.tensor_copy(stats_all[:, col:col + 1], stat[:, 5:6]))
                W("vector", S["vch"], S["vch"].n)
                sv2 = S["stat_"].inc()
                add("vector", lambda e, col=col: e.tensor_copy(stats_all[:, 32 + col:33 + col], stat[:, 4:5])
                    .then_inc(S["stat_"].h, 1))
            else:
                W("vector", S["xld"], v)
                W("vector", S["tr"], wtr)
                xnv = S["xn"].inc()
                add("vector", lambda e, col=col:
                    e.tensor_scalar(xn_tm[:, :], x_c[:, :], stats_all[:, 32 + col:33 + col],
                                    stats_all[:, col:col + 1], ALU.mult, ALU.subtract).then_inc(S["xn"].h, 1))
            if p1a:
                wdst = ("xp", S["xp"].n)
            else:
                wdst = ("guu", 64 * (k - 1)) if k >= 2 else ("guu", 0)
            for i in range(KC):
                t = tr_cnt[0]
                tr_cnt[0] += 1
                tv = S["tr"].inc(16)
                add("tensor", lambda e, i=i, t=t, xnv=xnv, wd=wdst:
                    (e.wait_ge(S["xn"].h, xnv),
                     e.wait_ge(S["trv"].h, t - 1) if t >= 1 else None,
                     e.wait_ge(S[wd[0]].h, wd[1]) if wd[1] else None,
                     e.transpose(ptr[t % 2][:, 0:128], xn_tm[:, i * 128:(i + 1) * 128], ident[:, :])
                     .then_inc(S["tr"].h, 16)))
                S["trv"].inc()
                d0 = i * QT + ch * 128
                fin = last_of_tile and i == KC - 1
                add("vector", lambda e, t=t, tv=tv, d0=d0, dst=xnT_dst, wd=wdst, fin=fin:
                    (e.wait_ge(S["tr"].h, tv),
                     e.wait_ge(S[wd[0]].h, wd[1]) if wd[1] else None,
                     (e.tensor_copy(dst[:, d0:d0 + 128], ptr[t % 2][:, 0:128]).then_inc(S["xnt"].h, 1)) if fin
                     else e.tensor_copy(dst[:, d0:d0 + 128], ptr[t % 2][:, 0:128]).then_inc(S["trv"].h, 1)))
                if fin:
                    S["xnt"].inc()
                    add("vector", lambda e: e.engine_nop().then_inc(S["trv"].h, 1))

        def emit_p1a_tile(k):
            xnT0 = fuse[0]
            for ch in range(4):
                emit_ln_chunk(k, ch, xnT0, p1a=True)
            xv = S["trv"].n
            wxpe = S["xpe"].n
            if os.environ.get("K_NOXPROJ") == "1":
                ctr_after_tile[k] = S["ctr"].n[k % 2]
                return
            for i in range(KC):
                last = i == KC - 1
                if last:
                    S["xp"].inc()
                add("tensor", lambda e, i=i, last=last, xv=xv, wxpe=wxpe:
                    (e.wait_ge(S["trv"].h, xv) if i == 0 else None,
                     e.wait_ge(S["xpe"].h, wxpe) if (i == 0 and wxpe) else None,
                     e.matmul(pxy[0][:, :], wxdt_s[:, i * 128:(i + 1) * 128], fuse[0][:, i * QT:(i + 1) * QT],
                              start=(i == 0), stop=last).then_inc(S["xp"].h, 1) if last else
                     e.matmul(pxy[0][:, :], wxdt_s[:, i * 128:(i + 1) * 128], fuse[0][:, i * QT:(i + 1) * QT],
                              start=(i == 0), stop=last)))
            for i in range(KC):
                last = i == KC - 1
                add("tensor", lambda e, i=i, last=last:
                    e.matmul(pxy[1][0:16, :], wxbm_s[:, i * 16:(i + 1) * 16], fuse[0][:, i * QT:(i + 1) * QT],
                             start=(i == 0), stop=last))
            for i in range(KC):
                last = i == KC - 1
                if last:
                    S["xp"].inc()
                add("tensor", lambda e, i=i, last=last:
                    e.matmul(pxy[1][32:48, :], wxcm_s[:, i * 16:(i + 1) * 16], fuse[0][:, i * QT:(i + 1) * QT],
                             start=(i == 0), stop=last).then_inc(S["xp"].h, 1) if last else
                    e.matmul(pxy[1][32:48, :], wxcm_s[:, i * 16:(i + 1) * 16], fuse[0][:, i * QT:(i + 1) * QT],
                             start=(i == 0), stop=last))
            vxp = S["xp"].n
            wctr = ctr_after_tile.get(k - 2, 0)
            e1 = S["xpe"].inc()
            add("vector", lambda e, vxp=vxp, k=k, w=wctr:
                (e.wait_ge(S["xp"].h, vxp - 1),
                 e.wait_ge(S["ctr"].h[k % 2], w) if w else None,
                 e.tensor_copy(dt_ev[k % 2][:, :], pxy[0][:, :]).then_inc(S["xpe"].h, 1)))
            e2 = S["xpe"].inc()
            add("vector", lambda e, vxp=vxp, k=k:
                (e.wait_ge(S["xp"].h, vxp),
                 e.tensor_copy(bc_ev[k % 2][0:16, :], pxy[1][0:16, :]).then_inc(S["xpe"].h, 1)))
            e3 = S["xpe"].inc()
            add("vector", lambda e, k=k:
                e.tensor_copy(bc_ev[k % 2][32:48, :], pxy[1][32:48, :]).then_inc(S["xpe"].h, 1))
            for (we, rows, src) in ((e1, (0, 128), dt_ev[k % 2]), (e2, (128, 144), None), (e3, (144, 160), None)):
                S["ctr"].inc(k % 2)
                if src is not None:
                    add("sync", lambda e, k=k, we=we, rows=rows, src=src:
                        (e.wait_ge(S["xpe"].h, we),
                         e.dma_start(out=cc1_in.ap()[rows[0]:rows[1], k * QT:(k + 1) * QT], in_=src[:, :])
                         .then_inc(S["ctr"].h[k % 2], 16)))
                elif rows[0] == 128:
                    add("sync", lambda e, k=k, we=we:
                        (e.wait_ge(S["xpe"].h, we),
                         e.dma_start(out=cc1_in.ap()[128:144, k * QT:(k + 1) * QT], in_=bc_ev[k % 2][0:16, :])
                         .then_inc(S["ctr"].h[k % 2], 16)))
                else:
                    add("sync", lambda e, k=k, we=we:
                        (e.wait_ge(S["xpe"].h, we),
                         e.dma_start(out=cc1_in.ap()[144:160, k * QT:(k + 1) * QT], in_=bc_ev[k % 2][32:48, :])
                         .then_inc(S["ctr"].h[k % 2], 16)))
            ctr_after_tile[k] = S["ctr"].n[k % 2]

        def emit_p1b_tile(k):
            for ch in range(4):
                emit_ln_chunk(k, ch, xnT[k % 2], p1a=False, last_of_tile=(ch == 3))

        def emit_ag1():
            v0, v1 = S["ctr"].n[0], S["ctr"].n[1]
            S["ag1"].inc()
            add("gpsimd", lambda e, v0=v0, v1=v1:
                (e.wait_ge(S["ctr"].h[0], v0), e.wait_ge(S["ctr"].h[1], v1) if v1 else None,
                 e.collective_compute("AllGather", ALU.bypass, replica_groups=GROUPS,
                                      ins=[cc1_in.ap().opt()], outs=[cc1_out.ap().opt()])
                 .then_inc(S["ag1"].h, 1)))

        def dn_pass(k, p):
            gp = 4 * k + p
            fq = p % 2
            wfu = S["fu"].n
            for db in range(DB):
                gdb = gp * DB + db
                S["wdl"].inc(gdb % 2)
                add("gpsimd", lambda e, db=db, p=p, gdb=gdb, w=max(0, gdb - 1):
                    (e.wait_ge(S["dn"].h, w) if w else None,
                     e.dma_start(out=wd_s[gdb % 2][:, :], in_=d_wd[db][:, p * 2048:(p + 1) * 2048])
                     .then_inc(S["wdl"].h[gdb % 2], 16)))
                vwd = S["wdl"].n[gdb % 2]
                for i in range(16):
                    last = i == 15
                    if last:
                        S["dn"].inc()
                    add("tensor", lambda e, i=i, gdb=gdb, fq=fq, last=last, vwd=vwd,
                        wpe=max(0, gdb - 1), wfu=wfu:
                        (e.wait_ge(S["wdl"].h[gdb % 2], vwd) if i == 0 else None,
                         e.wait_ge(S["fu"].h, wfu) if i == 0 else None,
                         e.wait_ge(S["pe_"].h, wpe) if (i == 0 and wpe) else None,
                         e.matmul(pdn[gdb % 2][:, :], wd_s[gdb % 2][:, i * 128:(i + 1) * 128],
                                  fuse[fq][:, i * QT:(i + 1) * QT], start=(i == 0), stop=last)
                         .then_inc(S["dn"].h, 1) if last else
                         e.matmul(pdn[gdb % 2][:, :], wd_s[gdb % 2][:, i * 128:(i + 1) * 128],
                                  fuse[fq][:, i * QT:(i + 1) * QT], start=(i == 0), stop=last)))
                vdn = S["dn"].n
                if p == 0:
                    wst = S["pst"].n[gdb % 2]
                    wld = S["pld"].n[gdb % 2]
                    S["pe_"].inc()
                    add("vector", lambda e, gdb=gdb, vdn=vdn, wst=wst:
                        (e.wait_ge(S["dn"].h, vdn),
                         e.wait_ge(S["pst"].h[gdb % 2], wst) if wst else None,
                         e.tensor_copy(pstg[gdb % 2][:, :], pdn[gdb % 2][:, :]).then_inc(S["pe_"].h, 1)))
                    S["pst"].inc(gdb % 2)
                    pst_after[gdb] = S["pst"].n[gdb % 2]
                    add("sync", lambda e, db=db, gdb=gdb, wpe=S["pe_"].n, wld=wld:
                        (e.wait_ge(S["pe_"].h, wpe),
                         e.wait_ge(S["pld"].h[gdb % 2], wld) if wld else None,
                         e.dma_start(out=d_phbm.ap()[db * 128:(db + 1) * 128, :], in_=pstg[gdb % 2][:, :])
                         .then_inc(S["pst"].h[gdb % 2], 16)))
                else:
                    wst = S["pst"].n[gdb % 2]
                    S["pld"].inc(gdb % 2)
                    pld_after[gdb] = S["pld"].n[gdb % 2]
                    add("sync", lambda e, db=db, gdb=gdb, wst=wst, wpe=max(0, gdb - 1):
                        (e.wait_ge(S["pst"].h[gdb % 2], wst),
                         e.wait_ge(S["pe_"].h, wpe) if wpe else None,
                         e.dma_start(out=pldg[gdb % 2][:, :], in_=d_phbm.ap()[db * 128:(db + 1) * 128, :])
                         .then_inc(S["pld"].h[gdb % 2], 16)))
                    if p < 3:
                        wst2 = S["pst"].n[gdb % 2]
                        S["pe_"].inc()
                        add("vector", lambda e, gdb=gdb, vdn=vdn, wld=S["pld"].n[gdb % 2], wst=wst2:
                            (e.wait_ge(S["dn"].h, vdn), e.wait_ge(S["pld"].h[gdb % 2], wld),
                             e.wait_ge(S["pst"].h[gdb % 2], wst) if wst else None,
                             e.tensor_tensor(pstg[gdb % 2][:, :], pldg[gdb % 2][:, :], pdn[gdb % 2][:, :],
                                             ALU.add).then_inc(S["pe_"].h, 1)))
                        S["pst"].inc(gdb % 2)
                        pst_after[gdb] = S["pst"].n[gdb % 2]
                        add("sync", lambda e, db=db, gdb=gdb, wpe=S["pe_"].n:
                            (e.wait_ge(S["pe_"].h, wpe),
                             e.dma_start(out=d_phbm.ap()[db * 128:(db + 1) * 128, :], in_=pstg[gdb % 2][:, :])
                             .then_inc(S["pst"].h[gdb % 2], 16)))
                    else:
                        S["pe_"].inc()
                        add("vector", lambda e, gdb=gdb, db=db, vdn=vdn, wld=S["pld"].n[gdb % 2],
                            wgs=max(0, k * DB + db - 1):
                            (e.wait_ge(S["dn"].h, vdn), e.wait_ge(S["pld"].h[gdb % 2], wld),
                             e.wait_ge(S["gsil"].h, wgs) if wgs else None,
                             e.tensor_tensor(gpre[db % 2][:, :], pldg[gdb % 2][:, :], pdn[gdb % 2][:, :],
                                             ALU.add).then_inc(S["pe_"].h, 1)))
                        S["gsil"].inc()
                        wgd = S["gdma"].n[db % 2]
                        add("scalar", lambda e, db=db, wpe=S["pe_"].n, wgd=wgd:
                            (e.wait_ge(S["pe_"].h, wpe),
                             e.wait_ge(S["gdma"].h[db % 2], wgd) if wgd else None,
                             e.activation(g_st[db % 2][:, :], gpre[db % 2][:, :], AF.Silu,
                                          scale=sd_s[:, db:db + 1]).then_inc(S["gsil"].h, 1)))
                        S["gdma"].inc(db % 2)
                        gdma_after[k * DB + db] = S["gdma"].n[db % 2]
                        add("sync", lambda e, k=k, db=db, wg=S["gsil"].n:
                            (e.wait_ge(S["gsil"].h, wg),
                             e.dma_start(out=d_ghbm.ap()[(k * DB + db) * 128:(k * DB + db + 1) * 128, :],
                                         in_=g_st[db % 2][:, :]).then_inc(S["gdma"].h[db % 2], 16)))


        p2ld = {}

        def emit_p2_loads(cc):
            t0 = cc * QS
            r = cc // 8
            lc = t0 - r * 4096
            base = r * CCROWS
            v_dt = S["dtld"].inc(cc % 2)
            add("sync", lambda e, base=base, lc=lc, cc=cc, w=max(0, 4 * (cc - 1)):
                (e.wait_ge(S["ag1"].h, 1) if cc == 0 else None,
                 e.wait_ge(S["dtp"].h, w) if w else None,
                 e.dma_start(out=dtv[cc % 2][:, :], in_=cc1_out.ap()[base:base + 128, lc:lc + QS])
                 .then_inc(S["dtld"].h[cc % 2], 16)))
            wxnsl = max(0, 4 * (cc - 1))
            for row, dst in ((160, mrB), (161, rsB)):
                S["bcb"].inc(cc % 2)
                add("sync", lambda e, base=base, row=row, lc=lc, dst=dst, cc=cc, w=wxnsl:
                    (e.wait_ge(S["xnsl_"].h, w) if w else None,
                     e.dma_start(out=dst[cc % 2][:, :],
                                 in_=cc1_out.ap()[base + row:base + row + 1, lc:lc + QS].partition_broadcast(128))
                     .then_inc(S["bcb"].h[cc % 2], 16)))
            v_mr = S["bcb"].n[cc % 2]
            for n in range(NN):
                wbx = max(0, 64 * (cc - 1) + 48 + n + 1) if cc > 0 else 0
                S["bca"].inc(cc % 2)
                add("sync", lambda e, base=base, n=n, lc=lc, cc=cc, w=wbx:
                    (e.wait_ge(S["bx"].h, w) if w else None,
                     e.dma_start(out=BmA[:, n * QS:(n + 1) * QS],
                                 in_=cc1_out.ap()[base + 128 + n:base + 129 + n, lc:lc + QS].partition_broadcast(128))
                     .then_inc(S["bca"].h[cc % 2], 16)))
                S["bca"].inc(cc % 2)
                add("sync", lambda e, base=base, n=n, lc=lc, cc=cc, w=wbx:
                    (e.wait_ge(S["wmul"].h, w) if w else None,
                     e.dma_start(out=CmA[:, n * QS:(n + 1) * QS],
                                 in_=cc1_out.ap()[base + 144 + n:base + 145 + n, lc:lc + QS].partition_broadcast(128))
                     .then_inc(S["bca"].h[cc % 2], 16)))
            p2ld[cc] = (v_dt, v_mr, S["bca"].n[cc % 2])

        def emit_slot(k, p, cc):
            have_p2 = (cc is not None) and not os.environ.get("K_NOP2") == "1"
            have_p3 = (k is not None) and not os.environ.get("K_NOP3") == "1"

            if have_p2:
                t0 = cc * QS
                v_dt, v_mr, v_bc_all = p2ld[cc]
                vdtp0 = S["dtp"].n
                spbase = S["sp"].n
                wu_war = max(0, 4 * (cc - 1))

                def xt_load(j):
                    jx = cc * NJ + j
                    v = S["xt"].inc(jx % 2)
                    add("sync", lambda e, j=j, jx=jx, w=max(0, jx - 1):
                        (e.wait_ge(S["xnsl_"].h, w) if w else None,
                         e.dma_start(out=xT_s[jx % 2][:, :], in_=d_xdslT[j * 128:(j + 1) * 128, t0:t0 + QS])
                         .then_inc(S["xt"].h[jx % 2], 16)))
                    return jx % 2, v

                xtv = {}

                def dtp_j(j):
                    if cc == 0 and j < 2:
                        wfree = ("xpe", 24)
                    elif cc == 0:
                        wfree = ("sY", j - 1)
                    elif j < 2:
                        wfree = ("sY", 4 * (cc - 1) + 3 + j)
                    else:
                        wfree = ("sY", 4 * cc + (j - 1))
                    wsp = spbase + (2 * (j - 2) + 1) if j >= 2 else 0
                    S["dtp"].inc()
                    add("tensor", lambda e, j=j, cc=cc, wfree=wfree, wsp=wsp, v_dt=v_dt:
                        (e.wait_ge(S[wfree[0]].h, wfree[1]) if wfree[1] else None,
                         e.wait_ge(S["sp"].h, wsp) if wsp else None,
                         e.wait_ge(S["dtld"].h[cc % 2], v_dt) if j == 0 else None,
                         e.matmul(pxy[j % 2][:, :], wdtT_s[:, j * 128:(j + 1) * 128], dtv[cc % 2][:, :],
                                  start=True, stop=True).then_inc(S["dtp"].h, 1)))

                def sp_j(j):
                    S["sp"].inc()
                    add("scalar", lambda e, j=j, v=vdtp0 + j + 1:
                        (e.wait_ge(S["dtp"].h, v),
                         e.activation(esp[:, :], pxy[j % 2][:, :], AF.Exp, bias=bdt_s[:, j:j + 1])
                         .then_inc(S["sp"].h, 1)))
                    S["sp"].inc()
                    add("scalar", lambda e, j=j, w=(wu_war + j + 1 if cc > 0 else 0):
                        (e.wait_ge(S["u_"].h, w) if w else None,
                         e.activation(delta[:, j * QS:(j + 1) * QS], esp[:, :], AF.Ln, bias=1.0)
                         .then_inc(S["sp"].h, 1)))

                def prep_j(j):
                    xp_, xv_ = xtv[j]
                    jx = cc * NJ + j
                    add("vector", lambda e, pp=xp_, v=xv_: e.wait_ge(S["xt"].h[pp], v))
                    add("vector", lambda e, pp=cc % 2, v=v_mr: e.wait_ge(S["bcb"].h[pp], v))
                    add("vector", lambda e, pp=cc % 2, v=v_bc_all: e.wait_ge(S["bca"].h[pp], v))
                    add("vector", lambda e, jx=jx, cc=cc:
                        e.tensor_tensor(xtmp[:, :], xT_s[jx % 2][:, :], rsB[cc % 2][:, :], ALU.mult))
                    S["xnsl_"].inc()
                    add("vector", lambda e, j=j, cc=cc:
                        e.tensor_tensor(xnsl[:, j * QS:(j + 1) * QS], xtmp[:, :], mrB[cc % 2][:, :], ALU.subtract)
                        .then_inc(S["xnsl_"].h, 1))
                    W("vector", S["sp"], spbase + 2 * j + 2)
                    S["u_"].inc()
                    add("vector", lambda e, j=j:
                        e.tensor_tensor(u_s[:, j * QS:(j + 1) * QS], delta[:, j * QS:(j + 1) * QS],
                                        xnsl[:, j * QS:(j + 1) * QS], ALU.mult).then_inc(S["u_"].h, 1))

                def cell(j, n):
                    g = cc * 64 + j * 16 + n
                    idx = j * NN + n
                    wyfree = ("sp", spbase + 2 * j + 1)
                    wyfree2 = ("sY", 4 * cc + (j - 1)) if j >= 2 else None
                    S["ad"].inc()
                    add("scalar", lambda e, j=j, g=g, idx=idx, w=max(0, g - 1):
                        (e.wait_ge(S["scan"].h, w) if w else None,
                         e.activation(ad_s[g % 2][:, :], delta[:, j * QS:(j + 1) * QS], AF.Exp,
                                      scale=A_s[:, idx:idx + 1]).then_inc(S["ad"].h, 1)))
                    S["bx"].inc()
                    add("vector", lambda e, j=j, n=n, g=g:
                        e.tensor_tensor(bx_s[g % 2][:, :], u_s[:, j * QS:(j + 1) * QS],
                                        BmA[:, n * QS:(n + 1) * QS], ALU.mult).then_inc(S["bx"].h, 1))
                    S["scan"].inc()
                    add("vector", lambda e, g=g, idx=idx, wad=S["ad"].n, ww=max(0, g - 2):
                        (e.wait_ge(S["ad"].h, wad),
                         e.wait_ge(S["wmul"].h, ww) if ww else None,
                         e.tensor_tensor_scan(h_s[g % 3][:, :], ad_s[g % 2][:, :], bx_s[g % 2][:, :],
                                              hcar[:, idx:idx + 1], ALU.mult, ALU.add)
                         .then_inc(S["scan"].h, 1)))
                    add("vector", lambda e, g=g, idx=idx, wsc=S["scan"].n:
                        (e.wait_ge(S["scan"].h, wsc),
                         e.tensor_copy(hcar[:, idx:idx + 1], h_s[g % 3][:, QS - 1:QS])))
                    S["wmul"].inc()
                    add("gpsimd", lambda e, g=g, n=n, wsc=S["scan"].n, wyi=max(0, g - 1):
                        (e.wait_ge(S["scan"].h, wsc),
                         e.wait_ge(S["yi"].h, wyi) if wyi else None,
                         e.tensor_tensor(w_s[g % 2][:, :], h_s[g % 3][:, :],
                                         CmA[:, n * QS:(n + 1) * QS], ALU.mult).then_inc(S["wmul"].h, 1)))
                    S["yi"].inc()
                    add("tensor", lambda e, j=j, n=n, g=g, ww=S["wmul"].n,
                        wy=(wyfree if n == 0 else None),
                        wy2=(wyfree2 if n == 0 else None):
                        (e.wait_ge(S[wy[0]].h, wy[1]) if wy else None,
                         e.wait_ge(S[wy2[0]].h, wy2[1]) if wy2 else None,
                         e.wait_ge(S["wmul"].h, ww),
                         e.matmul(pxy[j % 2][:, :], ident[:, :], w_s[g % 2][:, :],
                                  start=(n == 0), stop=(n == NN - 1)).then_inc(S["yi"].h, 1)))

                def sy_j(j):
                    S["sY"].inc()
                    wsdma_war = max(0, 16 * (2 * cc + (1 if j >= 2 else 0)))
                    add("vector", lambda e, j=j, wy=S["yi"].n, wsd=wsdma_war:
                        (e.wait_ge(S["yi"].h, wy),
                         e.wait_ge(S["sdma"].h[j % 2], wsd) if wsd else None,
                         e.scalar_tensor_tensor(s_st[j % 2][:, :], xnsl[:, j * QS:(j + 1) * QS],
                                                D_s[:, j:j + 1], pxy[j % 2][:, :], ALU.mult, ALU.add)
                         .then_inc(S["sY"].h, 1)))
                    q2 = cc // 8
                    S["sdma"].inc(j % 2)
                    add("sync", lambda e, j=j, q2=q2, lc2=(cc % 8) * QS, ws=S["sY"].n:
                        (e.wait_ge(S["sY"].h, ws),
                         e.dma_start(out=ccs_in[q2].ap()[j * 128:(j + 1) * 128, lc2:lc2 + QS],
                                     in_=s_st[j % 2][:, :]).then_inc(S["sdma"].h[j % 2], 16)))

            if have_p3:
                gp = 4 * k + p
                fq = p % 2
                pend_fu = [None]

                def hb_load(K):
                    hb = 16 * p + K
                    ghb = HB * k + hb
                    S["wg"].inc(ghb % 2)
                    add("gpsimd", lambda e, hb=hb, ghb=ghb, w=max(0, ghb - 1):
                        (e.wait_ge(S["guu"].h, w) if w else None,
                         e.dma_start(out=wgu_s[ghb % 2][:, :], in_=d_wgu[hb]).then_inc(S["wg"].h[ghb % 2], 16)))
                    return hb, ghb, S["wg"].n[ghb % 2]

                def hb_g(K, hb, ghb, vwg):
                    for i in range(KC):
                        last = i == KC - 1
                        if last:
                            S["gug"].inc()
                        add("tensor", lambda e, i=i, ghb=ghb, k=k, last=last, vwg=vwg, wge=ghb:
                            (e.wait_ge(S["wg"].h[ghb % 2], vwg) if i == 0 else None,
                             e.wait_ge(S["xnt"].h, k + 1) if i == 0 else None,
                             e.wait_ge(S["ge"].h, wge) if (i == 0 and wge) else None,
                             e.matmul(pg[:, :], wgu_s[ghb % 2][:, i * 128:(i + 1) * 128],
                                      xnT[k % 2][:, i * QT:(i + 1) * QT], start=(i == 0), stop=last)
                             .then_inc(S["gug"].h, 1) if last else
                             e.matmul(pg[:, :], wgu_s[ghb % 2][:, i * 128:(i + 1) * 128],
                                      xnT[k % 2][:, i * QT:(i + 1) * QT], start=(i == 0), stop=last)))

                def hb_u(K, hb, ghb):
                    for i in range(KC):
                        last = i == KC - 1
                        if last:
                            S["guu"].inc()
                        add("tensor", lambda e, i=i, ghb=ghb, k=k, last=last, wue=ghb:
                            (e.wait_ge(S["ue"].h, wue) if (i == 0 and wue) else None,
                             e.matmul(pu[:, :], wgu_s[ghb % 2][:, 2048 + i * 128:2176 + i * 128],
                                      xnT[k % 2][:, i * QT:(i + 1) * QT], start=(i == 0), stop=last)
                             .then_inc(S["guu"].h, 1) if last else
                             e.matmul(pu[:, :], wgu_s[ghb % 2][:, 2048 + i * 128:2176 + i * 128],
                                      xnT[k % 2][:, i * QT:(i + 1) * QT], start=(i == 0), stop=last)))

                def hb_evac(K, hb, ghb):
                    S["ge"].inc()
                    add("scalar", lambda e, ghb=ghb, hb=hb, wg=S["gug"].n, wfu=max(0, ghb - 1):
                        (e.wait_ge(S["gug"].h, wg),
                         e.wait_ge(S["fu"].h, wfu) if wfu else None,
                         e.activation(gate_s[ghb % 2][:, :], pg[:, :], AF.Silu, scale=sg_s[:, hb:hb + 1])
                         .then_inc(S["ge"].h, 1)))
                    S["ue"].inc()
                    add("scalar", lambda e, ghb=ghb, hb=hb, wu=S["guu"].n, wfu=max(0, ghb - 1):
                        (e.wait_ge(S["guu"].h, wu),
                         e.wait_ge(S["fu"].h, wfu) if wfu else None,
                         e.mul(up_s[ghb % 2][:, :], pu[:, :], su_s[:, hb:hb + 1]).then_inc(S["ue"].h, 1)))
                    slot = hb % 16
                    wdn = max(0, 16 * (gp - 1))
                    wxp0 = 16 if gp == 0 else 0
                    S["fu"].inc()
                    def fu_fn(e, ghb=ghb, slot=slot, fq=fq, wge=S["ge"].n, wue=S["ue"].n,
                              wdn=wdn, wxp0=wxp0):
                        return (e.wait_ge(S["xp"].h, wxp0) if wxp0 else None,
                                e.wait_ge(S["ge"].h, wge), e.wait_ge(S["ue"].h, wue),
                                e.wait_ge(S["dn"].h, wdn) if wdn else None,
                                e.tensor_tensor(fuse[fq][:, slot * QT:(slot + 1) * QT], gate_s[ghb % 2][:, :],
                                                up_s[ghb % 2][:, :], ALU.mult).then_inc(S["fu"].h, 1))
                    pend_fu[0] = fu_fn

                def flush_fu():
                    if pend_fu[0] is not None:
                        add("gpsimd", pend_fu[0])
                        pend_fu[0] = None

            if have_p2:
                xtv[0] = xt_load(0)
                xtv[1] = xt_load(1)
                dtp_j(0)
                sp_j(0)
                dtp_j(1)
                sp_j(1)
            for K in range(16):
                j = K // 4
                if have_p2 and K % 4 == 0:
                    if j >= 2:
                        dtp_j(j)
                        sp_j(j)
                    if j + 2 < NJ:
                        xtv[j + 2] = xt_load(j + 2)
                    prep_j(j)
                if have_p3:
                    hb, ghb, vwg = hb_load(K)
                    flush_fu()
                    hb_g(K, hb, ghb, vwg)
                if have_p2:
                    nb = 4 * (K % 4)
                    cell(j, nb)
                    cell(j, nb + 1)
                if have_p3:
                    hb_u(K, hb, ghb)
                if have_p2:
                    cell(j, nb + 2)
                    cell(j, nb + 3)
                if have_p3:
                    hb_evac(K, hb, ghb)
                if have_p2 and K % 4 == 3:
                    sy_j(j)
            if have_p3:
                flush_fu()
                dn_pass(k, p)
            if have_p2 and cc % 8 == 7:
                q2 = cc // 8
                S["sgo"].inc()
                add("vector", lambda e, w0=S["sdma"].n[0], w1=S["sdma"].n[1]:
                    (e.wait_ge(S["sdma"].h[0], w0), e.wait_ge(S["sdma"].h[1], w1),
                     e.engine_nop().then_inc(S["sgo"].h, 1)))
                S["ags"].inc()
                add("gpsimd", lambda e, q2=q2, wsg=S["sgo"].n:
                    (e.wait_ge(S["sgo"].h, wsg),
                     e.collective_compute("AllGather", ALU.bypass, replica_groups=GROUPS,
                                          ins=[ccs_in[q2].ap().opt()],
                                          outs=[ccs_out.ap()[q2 * 2048:(q2 + 1) * 2048, :].opt()])
                     .then_inc(S["ags"].h, 1)))

        def emit_p4():
            P4ONLY = os.environ.get("K_P4ONLY") == "1"
            if not P4ONLY:
                S["ags"].inc()
                add("gpsimd", lambda e, w=S["ags"].n - 1, wl=S["ldc"].n:
                    (e.wait_ge(S["ags"].h, w), e.wait_ge(S["ldc"].h, wl),
                     e.collective_compute("AllReduce", ALU.add, replica_groups=GROUPS,
                                          ins=[bar_in.ap().opt()], outs=[bar_out.ap().opt()])
                     .then_inc(S["ags"].h, 1)))
            for part in range(4):
                S["smv"].inc(16)
                add("gpsimd", lambda e, part=part, w=(0 if P4ONLY else S["ags"].n):
                    (e.wait_ge(S["ags"].h, w) if (w and part == 0) else None,
                     e.dma_start(out=s_mine.ap()[part * 512:(part + 1) * 512, :],
                                 in_=ccs_out.ap()[bass.ds((e.partition_id() % 4) * 2048 + part * 512, 512), :])
                     .then_inc(S["smv"].h, 16)))
            W("sync", S["smv"], S["smv"].n)
            if not P4ONLY:
                W("vector", S["ags"], 4)
            P4N = int(os.environ.get("K_P4N", str(NT * DB)))
            for k in range(NT):
                for db in range(DB):
                    if k * DB + db >= P4N:
                        continue
                    i = (k * DB + db) % 2
                    kd = k * DB + db
                    S["sld"].inc(i)
                    add("sync", lambda e, k=k, db=db, i=i, wsg=max(0, kd - 1):
                        (e.wait_ge(S["sg_"].h, wsg) if wsg else None,
                         e.dma_start(out=s_ld[i][:, :],
                                     in_=s_mine.ap()[(db // 4) * 512 + (db % 4) * 128:(db // 4) * 512 + (db % 4) * 128 + 128,
                                                     k * QT:(k + 1) * QT]).then_inc(S["sld"].h[i], 16)))
                    S["gld"].inc(i)
                    add("sync", lambda e, k=k, db=db, i=i, kd=kd,
                        wg=(0 if P4ONLY else S["gdma"].n[i]), wsg=max(0, kd - 1):
                        (e.wait_ge(S["gdma"].h[i], wg) if wg else None,
                         e.wait_ge(S["sg_"].h, wsg) if wsg else None,
                         e.dma_start(out=g_ld[i][:, :],
                                     in_=d_ghbm.ap()[kd * 128:(kd + 1) * 128, :]).then_inc(S["gld"].h[i], 16)))
                    S["xtld"].inc(i)
                    add("sync", lambda e, k=k, db=db, i=i, wof=max(0, kd - 1):
                        (e.wait_ge(S["ofa"].h, wof) if wof else None,
                         e.dma_start(out=xT_ld[i][:, :],
                                     in_=d_xtokT[db * 128:(db + 1) * 128, k * QT:(k + 1) * QT])
                         .then_inc(S["xtld"].h[i], 16)))
                    S["sg_"].inc()
                    add("vector", lambda e, i=i, ws=S["sld"].n[i], wg=S["gld"].n[i]:
                        (e.wait_ge(S["sld"].h[i], ws), e.wait_ge(S["gld"].h[i], wg),
                         e.tensor_tensor(sg4[i][:, :], s_ld[i][:, :], g_ld[i][:, :], ALU.mult)
                         .then_inc(S["sg_"].h, 1)))
                    S["ofa"].inc()
                    wo = out_after.get(kd - 2, 0)
                    add("vector", lambda e, i=i, wx=S["xtld"].n[i], wo=wo:
                        (e.wait_ge(S["xtld"].h[i], wx),
                         e.wait_ge(S["out"].h[i], wo) if wo else None,
                         e.tensor_tensor(of_s[i][:, :], sg4[i][:, :], xT_ld[i][:, :], ALU.add)
                         .then_inc(S["ofa"].h, 1)))
                    S["out"].inc(i)
                    out_after[kd] = S["out"].n[i]
                    add("sync", lambda e, k=k, db=db, i=i, wo=S["ofa"].n:
                        (e.wait_ge(S["ofa"].h, wo),
                         e.dma_start(out=d_out[db * 128:(db + 1) * 128, k * QT:(k + 1) * QT],
                                     in_=of_s[i][:, :]).then_inc(S["out"].h[i], 16)))
            add("sync", lambda e, v0=S["out"].n[0], v1=S["out"].n[1]:
                (e.wait_ge(S["out"].h[0], v0), e.wait_ge(S["out"].h[1], v1)))
            if DBGOUT == "post2":
                S["out"].inc(0)
                add("gpsimd", lambda e: e.dma_start(out=d_dbg, in_=s_mine.ap()).then_inc(S["out"].h[0], 16))
                add("sync", lambda e, v=S["out"].n[0]: e.wait_ge(S["out"].h[0], v))
            if DBGOUT == "post":
                S["out"].inc(0)
                add("gpsimd", lambda e: e.dma_start(out=d_dbg[0:1024, :],
                                                    in_=ccs_out.ap()[0:1024, :]).then_inc(S["out"].h[0], 16))
                S["out"].inc(0)
                add("gpsimd", lambda e: e.dma_start(out=d_dbg[1024:2048, 0:512],
                                                    in_=d_ghbm.ap()[0:1024, :]).then_inc(S["out"].h[0], 16))
                add("sync", lambda e, v=S["out"].n[0]: e.wait_ge(S["out"].h[0], v))

        NTILES = int(os.environ.get("K_NTILES", str(NT)))
        NOAG = os.environ.get("K_NOAG") == "1"
        NOSTATS = os.environ.get("K_NOSTATS") == "1"
        NOP1B = os.environ.get("K_NOP1B") == "1"
        NOXPROJ = os.environ.get("K_NOXPROJ") == "1"
        emit_consts()
        if os.environ.get("K_P4ONLY") == "1":
            emit_p4()
            return finish(nc, ev)
        for k in range(NTILES):
            emit_p1a_tile(k)
        if not NOSTATS:
            sv = S["stat_"].inc()
            add("vector", lambda e: e.tensor_copy(stats_bf[:, :], stats_all[:, :]).then_inc(S["stat_"].h, 1))
            t = tr_cnt[0]
            tr_cnt[0] += 1
            tv = S["tr"].inc(16)
            add("tensor", lambda e, t=t, sv=sv:
                (e.wait_ge(S["stat_"].h, sv),
                 e.wait_ge(S["trv"].h, t - 1) if t >= 1 else None,
                 e.transpose(ptr[t % 2][0:64, 0:128], stats_bf[:, :], ident[:, :]).then_inc(S["tr"].h, 16)))
            S["trv"].inc()
            sv2 = S["stat_"].inc()
            add("vector", lambda e, t=t, tv=tv:
                (e.wait_ge(S["tr"].h, tv),
                 e.tensor_copy(stT_sb[:, :], ptr[t % 2][0:64, 0:128]).then_inc(S["trv"].h, 1)))
            add("vector", lambda e: e.engine_nop().then_inc(S["stat_"].h, 1))
            for rowblk, srcsl in ((160, (0, 32)), (161, (32, 64))):
                S["ctr"].inc(0)
                add("sync", lambda e, rowblk=rowblk, srcsl=srcsl, sv2=sv2:
                    (e.wait_ge(S["stat_"].h, sv2),
                     e.dma_start(out=cc1_in.ap()[rowblk:rowblk + 1, :].rearrange("o (b f) -> (o b) f", b=32),
                                 in_=stT_sb[srcsl[0]:srcsl[1], :]).then_inc(S["ctr"].h[0], 16)))
        if not NOAG:
            emit_ag1()
        if not NOP1B:
            emit_p1b_tile(0)
            emit_p1b_tile(1)
        STAGE = int(os.environ.get("K_STAGE", "9"))
        if STAGE <= 1:
            if not NOAG:
                add("sync", lambda e: e.wait_ge(S["ag1"].h, 1))
            add("vector", lambda e, v=S["xnt"].n: e.wait_ge(S["xnt"].h, v))
            add("vector", lambda e, v=S["trv"].n: e.wait_ge(S["trv"].h, v))
            add("sync", lambda e, v0=S["ctr"].n[0], v1=S["ctr"].n[1]:
                (e.wait_ge(S["ctr"].h[0], v0), e.wait_ge(S["ctr"].h[1], v1) if v1 else None))
            add("scalar", lambda e, v=S["rstd"].n: e.wait_ge(S["rstd"].h, v))
            if DBGOUT == "cc1i":
                S["out"].inc(0)
                add("sync", lambda e: e.dma_start(out=d_dbg[0:162, :], in_=cc1_in.ap()).then_inc(S["out"].h[0], 16))
                add("sync", lambda e: e.wait_ge(S["out"].h[0], 16))
            elif DBGOUT == "stats":
                S["out"].inc(0)
                add("gpsimd", lambda e: e.dma_start(out=d_dbg[0:128, 0:64], in_=stats_all[:, :]).then_inc(S["out"].h[0], 16))
                add("sync", lambda e: e.wait_ge(S["out"].h[0], 16))
            elif DBGOUT == "xnt":
                S["out"].inc(0)
                add("sync", lambda e: e.dma_start(out=d_dbg[0:128, :],
                                                  in_=fuse[0][:, 0:4096]).then_inc(S["out"].h[0], 16))
                S["out"].inc(0)
                add("sync", lambda e: e.dma_start(out=d_dbg[128:256, :],
                                                  in_=fuse[0][:, 4096:8192]).then_inc(S["out"].h[0], 16))
                add("sync", lambda e: e.wait_ge(S["out"].h[0], 32))
            return finish(nc, ev)
        NSLOTS = int(os.environ.get("K_NSLOTS", str(32 + OFFS)))
        NOP2 = os.environ.get("K_NOP2") == "1"
        NOP3 = os.environ.get("K_NOP3") == "1"
        if not NOP2:
            emit_p2_loads(0)
        for s in range(min(32 + OFFS, NSLOTS)):
            part = (divmod(s, 4) if s < 32 else None) if not NOP3 else None
            cc = s - OFFS
            emit_slot(part[0] if part else None, part[1] if part else 0,
                      cc if (0 <= cc < NCH) else None)
            if not NOP2 and 1 <= s - 3 < NCH:
                emit_p2_loads(s - 3)
            if part and part[1] == 3 and part[0] + 2 < NT:
                emit_p1b_tile(part[0] + 2)
        if STAGE <= 2:
            add("sync", lambda e, v0=S["gdma"].n[0], v1=S["gdma"].n[1]:
                (e.wait_ge(S["gdma"].h[0], v0) if v0 else None,
                 e.wait_ge(S["gdma"].h[1], v1) if v1 else None))
            add("sync", lambda e, v=S["ags"].n: e.wait_ge(S["ags"].h, v))
            if DBGOUT == "cc1":
                S["out"].inc(0)
                add("sync", lambda e: e.dma_start(out=d_dbg[0:648, :], in_=cc1_out.ap()).then_inc(S["out"].h[0], 16))
                add("sync", lambda e: e.wait_ge(S["out"].h[0], 16))
            elif DBGOUT == "s":
                S["out"].inc(0)
                add("sync", lambda e: e.dma_start(out=d_dbg, in_=ccs_out.ap()[0:2048, :]).then_inc(S["out"].h[0], 16))
                add("sync", lambda e: e.wait_ge(S["out"].h[0], 16))
            elif DBGOUT == "g7":
                S["out"].inc(0)
                add("sync", lambda e: e.dma_start(out=d_dbg[0:2048, 0:512],
                                                  in_=d_ghbm.ap()[7 * 2048:8 * 2048, :]).then_inc(S["out"].h[0], 16))
                add("sync", lambda e: e.wait_ge(S["out"].h[0], 16))
            elif DBGOUT == "sm":
                S["out"].inc(0)
                add("gpsimd", lambda e: e.dma_start(out=d_dbg, in_=ccs_out.ap()[0:2048, :]).then_inc(S["out"].h[0], 16))
                add("sync", lambda e: e.wait_ge(S["out"].h[0], 16))
            elif DBGOUT == "g":
                S["out"].inc(0)
                add("sync", lambda e: e.dma_start(out=d_dbg[0:2048, 0:512],
                                                  in_=d_ghbm.ap()[0:2048, :]).then_inc(S["out"].h[0], 16))
                add("sync", lambda e: e.wait_ge(S["out"].h[0], 16))
            return finish(nc, ev)
        emit_p4()
        return finish(nc, ev)

    return nc


def kernel(L0_attn_x, L0_attn_W_xproj, L0_attn_W_dt, L0_attn_b_dt, L0_attn_A, L0_attn_D,
           L0_ffn_w_gate, L0_ffn_w_up, L0_ffn_w_down, L0_ffn_s_gate, L0_ffn_s_up, L0_ffn_s_down):
    x = np.asarray(L0_attn_x, np.float32)
    Wx = np.asarray(L0_attn_W_xproj, np.float32)
    Wdt = np.asarray(L0_attn_W_dt, np.float32)
    bdt = np.asarray(L0_attn_b_dt, np.float32)
    A = np.asarray(L0_attn_A, np.float32)
    Dv = np.asarray(L0_attn_D, np.float32)
    wg = np.asarray(L0_ffn_w_gate, np.int8)
    wu = np.asarray(L0_ffn_w_up, np.int8)
    wdn = np.asarray(L0_ffn_w_down, np.int8)
    sgv = np.asarray(L0_ffn_s_gate, np.float32)
    suv = np.asarray(L0_ffn_s_up, np.float32)
    sdv = np.asarray(L0_ffn_s_down, np.float32)

    wg_t = wg.reshape(KC, 128, HB, 128).transpose(2, 1, 0, 3).reshape(HB, 128, 2048)
    wu_t = wu.reshape(KC, 128, HB, 128).transpose(2, 1, 0, 3).reshape(HB, 128, 2048)
    wgu_np = np.ascontiguousarray(np.concatenate([wg_t, wu_t], axis=2))
    wd_np = np.ascontiguousarray(wdn.reshape(HB, 128, DB, 128).transpose(2, 1, 0, 3).reshape(DB, 128, HH))
    WxT = np.ascontiguousarray(Wx.T)
    wxdt_np = np.ascontiguousarray(WxT[:, 0:128].reshape(KC, 128, 128))
    wxbm_np = np.ascontiguousarray(WxT[:, 128:144].reshape(KC, 128, 16))
    wxcm_np = np.ascontiguousarray(WxT[:, 144:160].reshape(KC, 128, 16))
    sg_r = np.ascontiguousarray(sgv.reshape(HB, 128).T)
    su_r = np.ascontiguousarray(suv.reshape(HB, 128).T)
    sd_r = np.ascontiguousarray(sdv.reshape(DB, 128).T)
    ident_np = np.eye(128, dtype=BF)

    in_maps = []
    for c in range(8):
        b, q = divmod(c, 4)
        dsl = slice(512 * q, 512 * (q + 1))
        in_maps.append({
            "x_tok": np.ascontiguousarray(x[b, q * TOK:(q + 1) * TOK, :]),
            "x_tokT": np.ascontiguousarray(x[b, q * TOK:(q + 1) * TOK, :].T),
            "x_dslT": np.ascontiguousarray(x[b, :, dsl].T),
            "wgu": wgu_np, "wd": wd_np,
            "wxdt": wxdt_np, "wxbm": wxbm_np, "wxcm": wxcm_np,
            "wdtT": np.ascontiguousarray(Wdt[dsl, :].T),
            "A_r": np.ascontiguousarray(A[dsl].reshape(NJ, 128, NN).transpose(1, 0, 2).reshape(128, NJ * NN)),
            "bdt_r": np.ascontiguousarray(bdt[dsl].reshape(NJ, 128).T),
            "D_r": np.ascontiguousarray(Dv[dsl].reshape(NJ, 128).T),
            "sg_r": sg_r, "su_r": su_r, "sd_r": sd_r,
            "ident": ident_np, "eps_c": np.full((128, 1), EPS, np.float32),
        })

    nc = build_program()
    res = run_bass_kernel_spmd(nc, in_maps, core_ids=list(range(8)))
    out = np.empty((BB, LL, DD), np.float32)
    for c in range(8):
        b, q = divmod(c, 4)
        out[b, q * TOK:(q + 1) * TOK, :] = res.results[c]["outT"].T
    return out
